# version 03
# baseline (speedup 1.0000x reference)
import numpy as np
from contextlib import ExitStack

import concourse.bass as bass
import concourse.mybir as mybir
from concourse.bass_utils import run_bass_kernel_spmd

F32 = mybir.dt.float32
BF16 = mybir.dt.bfloat16
AF = mybir.ActivationFunctionType
ALU = mybir.AluOpType
AX = mybir.AxisListType.X

P = 128
T = 1024
TH = 1027
D = 2048
KC = 16
E = 16
ALPHA = 1.702
INVA = float(np.float32(1.0) / np.float32(1.702))
EPS = 1e-5
BIG = 1.0e30
TQ = 256
NQ = 4


class Sem:
    def __init__(self, nc, st, name):
        self.h = st.enter_context(nc.semaphore(name))
        self.n = 0

    def inc(self, k=1):
        self.n += k
        return self.n


def build_program():
    nc = bass.Bass("TRN2", target_bir_lowering=False, debug=False)

    d_xh = nc.dram_tensor("xh", [TH, D], F32, kind="ExternalInput").ap()
    d_wi = nc.dram_tensor("wi", [64, P, KC, P], F32, kind="ExternalInput").ap()
    d_wo = nc.dram_tensor("wo", [16, P, 32, P], F32, kind="ExternalInput").ap()
    d_g = nc.dram_tensor("gw", [E, 16, P, KC, P], F32, kind="ExternalInput").ap()
    d_u = nc.dram_tensor("uw", [E, 16, P, KC, P], F32, kind="ExternalInput").ap()
    d_dn = nc.dram_tensor("dw", [E, 16, P, KC, P], F32, kind="ExternalInput").ap()
    d_rt = nc.dram_tensor("rt", [P, KC, E], F32, kind="ExternalInput").ap()
    d_rb = nc.dram_tensor("rb", [E, 1], F32, kind="ExternalInput").ap()
    d_db = nc.dram_tensor("db", [E, D], F32, kind="ExternalInput").ap()
    d_bg = nc.dram_tensor("bg", [P, 256], F32, kind="ExternalInput").ap()
    d_bu = nc.dram_tensor("bu", [P, 256], F32, kind="ExternalInput").ap()
    d_cw = nc.dram_tensor("cw", [P, P], F32, kind="ExternalInput").ap()
    d_cb = nc.dram_tensor("cb", [P, 32], F32, kind="ExternalInput").ap()
    d_dv = nc.dram_tensor("dv", [P, 32], F32, kind="ExternalInput").ap()
    d_y = nc.dram_tensor("y", [T, D], F32, kind="ExternalOutput").ap()

    st = ExitStack()
    e_ = st.enter_context

    xnT = e_(nc.sbuf_tensor("xnT", [P, KC, 1028], BF16))
    ACC = e_(nc.sbuf_tensor("ACC", [P, KC, T], F32))
    BSCR = e_(nc.sbuf_tensor("BSCR", [P, 16384], BF16))
    STRM = e_(nc.sbuf_tensor("STRM", [P, 20480], BF16))
    TMPP = e_(nc.sbuf_tensor("TMPP", [P, 10240], BF16))

    identb = e_(nc.sbuf_tensor("identb", [P, P], BF16))
    identf = e_(nc.sbuf_tensor("identf", [P, P], F32))
    rt_sb = e_(nc.sbuf_tensor("rt_sb", [P, KC, E], BF16))
    rb_sb = e_(nc.sbuf_tensor("rb_sb", [E, 1], F32))
    db_sb = e_(nc.sbuf_tensor("db_sb", [E, D], BF16))
    bg_sb = e_(nc.sbuf_tensor("bg_sb", [P, 256], F32))
    bu1_sb = e_(nc.sbuf_tensor("bu1_sb", [P, 256], F32))
    cw_sb = e_(nc.sbuf_tensor("cw_sb", [P, P], F32))
    cb_sb = e_(nc.sbuf_tensor("cb_sb", [P, 32], F32))
    dv_sb = e_(nc.sbuf_tensor("dv_sb", [P, 32], F32))
    sel_sb = e_(nc.sbuf_tensor("sel_sb", [E, 16 * P], BF16))
    eps_sb = e_(nc.sbuf_tensor("eps_sb", [P, 1], F32))
    ln_sc = e_(nc.sbuf_tensor("ln_sc", [P, 8], F32))
    lg_tm = e_(nc.sbuf_tensor("lg_tm", [P, 8, E], F32))
    sc_tm = e_(nc.sbuf_tensor("sc_tm", [P, 8, E], F32))
    tkw = e_(nc.sbuf_tensor("tkw", [P, E], F32))
    tkm = e_(nc.sbuf_tensor("tkm", [P, 8], F32))
    tke = e_(nc.sbuf_tensor("tke", [P, E], F32))
    tke2 = e_(nc.sbuf_tensor("tke2", [P, E], F32))
    scTb = e_(nc.sbuf_tensor("scTb", [E, T], BF16))
    sel_v = sel_sb[:].rearrange("p (a b) -> p a b", a=16)

    PSA = e_(nc.psum_tensor("PSA", [P, 2048], F32))
    PSB = e_(nc.psum_tensor("PSB", [P, 2048], F32))

    def pb(i):
        t = PSA if i < 4 else PSB
        j = i % 4
        return t[:, j * 512:(j + 1) * 512]

    def pbb(i):
        t = (PSA if i < 4 else PSB)[:].bitcast(BF16)
        j = i % 4
        return t[:, j * 1024:(j + 1) * 1024]

    def p2(i):
        t = PSA if i < 4 else PSB
        j = i % 4
        return t[:, j * 512:(j + 2) * 512]

    def carve(base, byte_off, dtype, n0, n1=None):
        esz = 4 if dtype == F32 else 2
        nel = n0 * (n1 or 1)
        v = base[:, byte_off // 2:(byte_off + nel * esz) // 2]
        if dtype == F32:
            v = v.bitcast(F32)
        if n1 is not None:
            v = v.rearrange("p (a b) -> p a b", a=n0)
        return v

    act_v = carve(BSCR, 0, BF16, 16, 1024)
    h_v = carve(BSCR, 0, BF16, 32, 256)
    sg_v = carve(BSCR, 16384, F32, 16, 256)

    xt_v = [carve(STRM, 0, F32, 2048), carve(STRM, 8192, F32, 2048)]
    xn_v = [carve(STRM, 16384, BF16, 2048), carve(STRM, 20480, BF16, 2048)]
    gbuf_v = [carve(STRM, 0, BF16, 16, 128), carve(STRM, 4096, BF16, 16, 128)]
    ubuf_v = [carve(STRM, 8192, BF16, 16, 128), carve(STRM, 12288, BF16, 16, 128)]
    dwbuf_v = [carve(STRM, 16384, BF16, 16, 128), carve(STRM, 20480, BF16, 16, 128)]
    bcast_v = [carve(STRM, 24576, F32, 1024), carve(STRM, 28672, F32, 1024)]
    wibuf_v = [carve(STRM, 0, BF16, 16, 128), carve(STRM, 4096, BF16, 16, 128)]
    wobuf_v = [carve(STRM, 8192, BF16, 32, 128), carve(STRM, 16384, BF16, 32, 128)]
    xres_v = [carve(STRM, 24576, F32, 2048), carve(STRM, 32768, F32, 2048)]

    xcen_v = carve(TMPP, 0, F32, 2048)
    sq_v = carve(TMPP, 8192, F32, 2048)
    gc_v = [carve(TMPP, 0, F32, 512), carve(TMPP, 8192, F32, 512)]
    glu_v = [carve(TMPP, 2048, F32, 512), carve(TMPP, 10240, F32, 512)]
    u3_v = [carve(TMPP, 4096, F32, 512), carve(TMPP, 12288, F32, 512)]
    a1_v = [carve(TMPP, 6144, F32, 512), carve(TMPP, 14336, F32, 512)]
    cv0_v = [carve(TMPP, 0, F32, 256), carve(TMPP, 4096, F32, 256)]
    cv1_v = [carve(TMPP, 1024, F32, 256), carve(TMPP, 5120, F32, 256)]
    sx_v = [carve(TMPP, 2048, F32, 256), carve(TMPP, 6144, F32, 256)]
    sgx_v = [carve(TMPP, 3072, F32, 256), carve(TMPP, 7168, F32, 256)]
    y_v = carve(TMPP, 8192, F32, 2048)
    sil_v = [carve(TMPP, 16384, F32, 256), carve(TMPP, 17408, F32, 256)]
    lgT_sb = carve(TMPP, 0, F32, 1024)[0:16, :]

    qP = Sem(nc, st, "qP")
    qPg = Sem(nc, st, "qPg")
    qX = [Sem(nc, st, "qXa"), Sem(nc, st, "qXb")]
    qG = [Sem(nc, st, "qGa"), Sem(nc, st, "qGb")]
    qU = [Sem(nc, st, "qUa"), Sem(nc, st, "qUb")]
    qD = [Sem(nc, st, "qDa"), Sem(nc, st, "qDb")]
    qW = [Sem(nc, st, "qWa"), Sem(nc, st, "qWb")]
    qO = [Sem(nc, st, "qOa"), Sem(nc, st, "qOb")]
    qR = [Sem(nc, st, "qRa"), Sem(nc, st, "qRb")]
    qY = Sem(nc, st, "qY")
    sT = Sem(nc, st, "sT")
    sV = Sem(nc, st, "sV")
    sS = Sem(nc, st, "sS")
    sG = Sem(nc, st, "sG")
    sVa = Sem(nc, st, "sVa")
    sSa = Sem(nc, st, "sSa")

    OPS = {k: [] for k in ("t", "v", "s", "q", "g")}

    def em(k, fn):
        OPS[k].append(fn)

    def g0(g):
        g.memset(identb[:], 0.0).then_inc(sG.h, 1)
        g.memset(identf[:], 0.0).then_inc(sG.h, 1)
        g.memset(sel_sb[:], 0.0).then_inc(sG.h, 1)
        g.memset(eps_sb[:], EPS).then_inc(sG.h, 1)
        g.wait_ge(sG.h, 4)
        g.affine_select(out=identb[:], in_=identb[:], compare_op=ALU.not_equal,
                        fill=1.0, base=0, pattern=[[-1, P]],
                        channel_multiplier=1).then_inc(sG.h, 1)
        g.affine_select(out=identf[:], in_=identf[:], compare_op=ALU.not_equal,
                        fill=1.0, base=0, pattern=[[-1, P]],
                        channel_multiplier=1).then_inc(sG.h, 1)
        g.affine_select(out=sel_v[:, :, :], in_=sel_v[:, :, :],
                        compare_op=ALU.not_equal, fill=1.0, base=0,
                        pattern=[[-1, 16], [0, P]],
                        channel_multiplier=1).then_inc(sG.h, 1)
        g.dma_start(out=rt_sb[:], in_=d_rt).then_inc(qPg.h, 16)
        g.dma_start(out=db_sb[:], in_=d_db).then_inc(qPg.h, 16)
    em("g", g0)
    sG.inc(6)
    GCONST = sG.inc(1)
    qPg.inc(32)

    def q0(q):
        q.dma_start(out=rb_sb[:], in_=d_rb).then_inc(qP.h, 16)
        q.dma_start(out=bg_sb[:], in_=d_bg).then_inc(qP.h, 16)
        q.dma_start(out=bu1_sb[:], in_=d_bu).then_inc(qP.h, 16)
        q.dma_start(out=cw_sb[:], in_=d_cw).then_inc(qP.h, 16)
        q.dma_start(out=cb_sb[:], in_=d_cb).then_inc(qP.h, 16)
        q.dma_start(out=dv_sb[:], in_=d_dv).then_inc(qP.h, 16)
    em("q", q0)
    qP.inc(96)
    P_ALL = qP.n
    PG_ALL = qPg.n

    def v0(v, w=P_ALL):
        v.wait_ge(qP.h, w)
        v.tensor_scalar(out=bu1_sb[:], in0=bu1_sb[:], scalar1=1.0, scalar2=None,
                        op0=ALU.add)
    em("v", v0)

    nrows = [P] * 8 + [3]
    v_xn = {}
    v_ev = {}
    t_tr = {}
    for j in range(9):
        par = j % 2
        r = nrows[j]

        free = v_xn.get(j - 2)

        def qld(q, j=j, par=par, r=r, free=free):
            if free is not None:
                q.wait_ge(sV.h, free)
            q.dma_start(out=xt_v[par][0:r, :], in_=d_xh[j * P:j * P + r, :]
                        ).then_inc(qX[par].h, 16)
        em("q", qld)
        wx = qX[par].inc(16)

        a1 = sV.inc(1)
        a2 = sV.inc(1)

        def vln(v, par=par, r=r, wx=wx, a1=a1):
            v.wait_ge(qX[par].h, wx)
            xt = xt_v[par][0:r, :]
            v.reduce_sum(ln_sc[0:r, 0:1], xt, axis=AX).then_inc(sV.h, 1)
            v.wait_ge(sV.h, a1)
            v.tensor_scalar(out=ln_sc[0:r, 1:2], in0=ln_sc[0:r, 0:1],
                            scalar1=1.0 / D, scalar2=None, op0=ALU.mult)
            v.tensor_scalar(out=xcen_v[0:r, :], in0=xt, scalar1=ln_sc[0:r, 1:2],
                            scalar2=None, op0=ALU.subtract)
            v.tensor_tensor(out=sq_v[0:r, :], in0=xcen_v[0:r, :],
                            in1=xcen_v[0:r, :], op=ALU.mult)
            v.reduce_sum(ln_sc[0:r, 2:3], sq_v[0:r, :], axis=AX).then_inc(sV.h, 1)
        em("v", vln)

        b1 = sS.inc(1)
        b2 = sS.inc(1)

        def sln(s, r=r, a2=a2, b1=b1, j=j):
            if j == 0:
                s.wait_ge(sG.h, GCONST)
                s.wait_ge(qP.h, P_ALL)
            s.wait_ge(sV.h, a2)
            s.activation(out=ln_sc[0:r, 3:4], in_=ln_sc[0:r, 2:3], func=AF.Ln,
                         bias=eps_sb[0:r, :], scale=1.0 / D).then_inc(sS.h, 1)
            s.wait_ge(sS.h, b1)
            s.activation(out=ln_sc[0:r, 4:5], in_=ln_sc[0:r, 3:4], func=AF.Exp,
                         bias=0.0, scale=-0.5).then_inc(sS.h, 1)
        em("s", sln)

        c1 = sV.inc(1)
        v_xn[j] = c1
        ttr_prev = t_tr.get(j - 2)

        def vxn(v, par=par, r=r, b2=b2, ttr_prev=ttr_prev):
            v.wait_ge(sS.h, b2)
            if ttr_prev is not None:
                v.wait_ge(sT.h, ttr_prev)
            v.tensor_scalar(out=xn_v[par][0:r, :], in0=xcen_v[0:r, :],
                            scalar1=ln_sc[0:r, 4:5], scalar2=None,
                            op0=ALU.mult).then_inc(sV.h, 1)
        em("v", vxn)

        d1 = sT.inc(1)
        t_tr[j] = d1
        ev_prev = v_ev.get(j - 1)

        def ttr(t, par=par, r=r, c1=c1, ev_prev=ev_prev):
            t.wait_ge(sV.h, c1)
            t.wait_ge(sG.h, GCONST)
            if ev_prev is not None:
                t.wait_ge(sV.h, ev_prev)
            xn3 = xn_v[par].rearrange("p (a b) -> p a b", a=KC)
            blk = r if r % 2 == 0 else r + 1
            last = None
            for k in range(KC):
                bank = 0 if k < 8 else 1
                dst = pbb(bank)[:, (k % 8) * blk:(k % 8) * blk + r]
                last = t.transpose(dst, xn3[0:r, k, :], identb[0:r, 0:r])
            last.then_inc(sT.h, 1)
        em("t", ttr)

        e1 = sV.inc(1)
        v_ev[j] = e1

        def vev(v, j=j, r=r, d1=d1):
            v.wait_ge(sT.h, d1)
            blk = r if r % 2 == 0 else r + 1
            for bank in range(2):
                src = pbb(bank)[:, 0:8 * blk].rearrange("p (a b) -> p a b", a=8)
                dst = xnT[:, bank * 8:bank * 8 + 8, j * P:j * P + r]
                v.tensor_copy(dst, src[:, :, 0:r])
            v.nop().then_inc(sV.h, 1)
        em("v", vev)

    V_XN8 = v_xn[8]
    V_EV8 = v_ev[8]

    rT = sT.inc(1)

    def t_lg(t, ev8=V_EV8, pall=PG_ALL):
        t.wait_ge(sV.h, ev8)
        t.wait_ge(qPg.h, pall)
        last = None
        for c2 in range(2):
            dst = pb(2 + c2)[0:16, :]
            for k in range(KC):
                last = t.matmul(dst, rt_sb[:, k, :],
                                xnT[:, k, 3 + c2 * 512:3 + c2 * 512 + 512],
                                start=(k == 0), stop=(k == KC - 1))
        last.then_inc(sT.h, 1)
    em("t", t_lg)

    rV = sV.inc(1)

    def v_lg(v, rT=rT):
        v.wait_ge(sT.h, rT)
        v.tensor_scalar(out=lgT_sb[0:16, :], in0=PSA[0:16, 1024:2048],
                        scalar1=rb_sb[0:16, :], scalar2=None,
                        op0=ALU.add).then_inc(sV.h, 1)
    em("v", v_lg)

    rT2 = sT.inc(1)

    def t_lgt(t, rV=rV):
        t.wait_ge(sV.h, rV)
        last = None
        for j8 in range(8):
            last = t.transpose(pb(4)[:, j8 * 16:(j8 + 1) * 16],
                               lgT_sb[0:16, j8 * P:(j8 + 1) * P], identf[0:16, 0:16])
        last.then_inc(sT.h, 1)
    em("t", t_lgt)

    rV2 = sV.inc(1)

    def v_lgt(v, rT2=rT2):
        v.wait_ge(sT.h, rT2)
        v.tensor_copy(lg_tm[:, :, :], pb(4)[:, 0:P].rearrange("p (a b) -> p a b", a=8))
        v.nop().then_inc(sV.h, 1)
    em("v", v_lgt)

    sc_done_v = None
    for j8 in range(8):
        edges = [sV.inc(1) for _ in range(4)]
        mdone = edges[3]

        def v_tk1(v, j8=j8, rV2=rV2, edges=edges):
            if j8 == 0:
                v.wait_ge(sV.h, rV2)
            v.tensor_copy(tkw[:], lg_tm[:, j8, :])
            v.reduce_max(tkm[:, 4:5], tkw[:], axis=AX, negate=True).then_inc(sV.h, 1)
            v.wait_ge(sV.h, edges[0])
            v.tensor_scalar(out=tkm[:, 0:1], in0=tkm[:, 4:5], scalar1=-1.0,
                            scalar2=None, op0=ALU.mult)
            for i in range(1, 4):
                v.tensor_scalar(out=tke[:], in0=tkw[:], scalar1=tkm[:, i - 1:i],
                                scalar2=None, op0=ALU.is_ge)
                v.scalar_tensor_tensor(out=tkw[:], in0=tke[:], scalar=-BIG,
                                       in1=tkw[:], op0=ALU.mult, op1=ALU.add)
                v.reduce_max(tkm[:, i:i + 1], tkw[:], axis=AX).then_inc(sV.h, 1)
                v.wait_ge(sV.h, edges[i])
            v.tensor_scalar(out=tke[:], in0=lg_tm[:, j8, :], scalar1=tkm[:, 3:4],
                            scalar2=None, op0=ALU.is_ge)
            v.nop().then_inc(sV.h, 1)
        mask_done = sV.inc(1)
        em("v", v_tk1)

        exp_done = sS.inc(1)

        def s_tk(s, j8=j8, mask_done=mask_done):
            s.wait_ge(sV.h, mask_done)
            s.activation(out=tke2[:], in_=lg_tm[:, j8, :], func=AF.Exp,
                         bias=tkm[:, 4:5], scale=1.0).then_inc(sS.h, 1)
        em("s", s_tk)

        eden = sV.inc(1)
        erden = sV.inc(1)
        sc_done_v = sV.inc(1)

        def v_tk2(v, j8=j8, exp_done=exp_done, eden=eden, erden=erden):
            v.wait_ge(sS.h, exp_done)
            v.tensor_tensor(out=tke2[:], in0=tke2[:], in1=tke[:], op=ALU.mult)
            v.reduce_sum(tkm[:, 5:6], tke2[:], axis=AX).then_inc(sV.h, 1)
            v.wait_ge(sV.h, eden)
            v.reciprocal(tkm[:, 6:7], tkm[:, 5:6]).then_inc(sV.h, 1)
            v.wait_ge(sV.h, erden)
            v.tensor_scalar(out=sc_tm[:, j8, :], in0=tke2[:], scalar1=tkm[:, 6:7],
                            scalar2=None, op0=ALU.mult).then_inc(sV.h, 1)
        em("v", v_tk2)

    SC_TM_DONE = sc_done_v

    rT3 = sT.inc(1)

    def t_sc(t, w=SC_TM_DONE):
        t.wait_ge(sV.h, w)
        last = None
        for j8 in range(8):
            bank = 5 if j8 < 4 else 6
            last = t.transpose(pb(bank)[0:16, (j8 % 4) * 128:(j8 % 4) * 128 + 128],
                               sc_tm[:, j8, :], identf[:])
        last.then_inc(sT.h, 1)
    em("t", t_sc)

    SCT_DONE = sV.inc(1)

    def v_sc(v, rT3=rT3):
        v.wait_ge(sT.h, rT3)
        v.tensor_copy(scTb[0:16, 0:512], pb(5)[0:16, :])
        v.tensor_copy(scTb[0:16, 512:1024], pb(6)[0:16, :])
        v.nop().then_inc(sV.h, 1)
    em("v", v_sc)

    acc_init_t = {}
    acc_init_v = {}
    for hs in range(KC):
        bank0 = (hs % 2) * 2
        tn = sT.inc(1)
        acc_init_t[hs] = tn
        prev_ev = acc_init_v.get(hs - 2)

        def t_ai(t, hs=hs, bank0=bank0, prev_ev=prev_ev, w=SCT_DONE):
            if hs == 0:
                t.wait_ge(sV.h, w)
            if prev_ev is not None:
                t.wait_ge(sV.h, prev_ev)
            last = None
            for c2 in range(2):
                last = t.matmul(pb(bank0 + c2),
                                db_sb[0:16, hs * P:(hs + 1) * P],
                                scTb[0:16, c2 * 512:(c2 + 1) * 512],
                                start=True, stop=True)
            last.then_inc(sT.h, 1)
        em("t", t_ai)

        vn = sV.inc(1)
        acc_init_v[hs] = vn

        def v_ai(v, hs=hs, bank0=bank0, tn=tn):
            v.wait_ge(sT.h, tn)
            v.tensor_copy(ACC[:, hs, :], p2(bank0))
            v.nop().then_inc(sV.h, 1)
        em("v", v_ai)
    ACC_INIT_V = acc_init_v[15]
    ACC_INIT_V14 = acc_init_v[14]

    pe_slice_done = {}
    v_act_done = {}
    v_down_done = {}
    pe_down_done = {}
    v_bcast_done = {}
    t_bcast_done = {}
    s_glu_done = {}
    act_all = {}

    for e in range(E):
        tb = sT.inc(1)
        t_bcast_done[e] = tb
        w_prev_odd = v_down_done.get((e - 1, 15))

        def t_bc(t, e=e, w_prev_odd=w_prev_odd, wsc=SCT_DONE):
            if e == 0:
                t.wait_ge(sV.h, wsc)
            if w_prev_odd is not None:
                t.wait_ge(sV.h, w_prev_odd)
            last = None
            for c2 in range(2):
                last = t.matmul(pb(6 + c2), sel_v[:, e, :],
                                scTb[0:16, c2 * 512:(c2 + 1) * 512],
                                start=True, stop=True)
            last.then_inc(sT.h, 1)
        em("t", t_bc)

        vb = sV.inc(1)
        v_bcast_done[e] = vb

        def v_bc(v, e=e, tb=tb):
            v.wait_ge(sT.h, tb)
            v.tensor_scalar(out=bcast_v[e % 2], in0=PSB[:, 1024:2048],
                            scalar1=INVA, scalar2=None, op0=ALU.mult)
            v.nop().then_inc(sV.h, 1)
        em("v", v_bc)

        for m in range(KC):
            pm = m % 2
            gate_free = pe_slice_done.get((e, m - 2)) if m >= 2 else \
                pe_slice_done.get((e - 1, 14 + m))

            def g_gu(g, e=e, m=m, pm=pm, gate_free=gate_free, vxn8=V_XN8):
                if e == 0 and m < 2:
                    g.wait_ge(sV.h, vxn8)
                if gate_free is not None:
                    g.wait_ge(sT.h, gate_free)
                g.dma_start(out=gbuf_v[pm][:], in_=d_g[e, m]).then_inc(qG[pm].h, 16)
                g.dma_start(out=ubuf_v[pm][:], in_=d_u[e, m]).then_inc(qU[pm].h, 16)
            em("g", g_gu)
            wg = qG[pm].inc(16)
            wu = qU[pm].inc(16)

            for tc in range(2):
                sset = (2 * m + tc) % 2
                gbank, ubank = (0, 1) if sset == 0 else (2, 3)
                idx = 2 * m + tc
                if e == 0 and idx < 2:
                    set_free = ACC_INIT_V14 if idx == 0 else ACC_INIT_V
                else:
                    set_free = v_act_done.get((e, idx - 2)) or act_all[e - 1]

                tn = sT.inc(1)
                if tc == 1:
                    pe_slice_done[(e, m)] = tn

                def t_gu(t, e=e, m=m, pm=pm, tc=tc, gbank=gbank, ubank=ubank,
                         wg=wg, wu=wu, set_free=set_free, vb=vb):
                    if tc == 0:
                        t.wait_ge(qG[pm].h, wg)
                        t.wait_ge(qU[pm].h, wu)
                    t.wait_ge(sV.h, set_free)
                    if m == 0 and tc == 0:
                        t.wait_ge(sV.h, vb)
                    rhs0 = 3 + tc * 512
                    last = None
                    for k in range(KC):
                        t.matmul(pb(gbank), gbuf_v[pm][:, k, :],
                                 xnT[:, k, rhs0:rhs0 + 512],
                                 start=(k == 0), stop=(k == KC - 1))
                        last = t.matmul(pb(ubank), ubuf_v[pm][:, k, :],
                                        xnT[:, k, rhs0:rhs0 + 512],
                                        start=(k == 0), stop=(k == KC - 1))
                    last.then_inc(sT.h, 1)
                em("t", t_gu)

                vgc = sV.inc(1)

                def v_gc(v, e=e, m=m, tc=tc, sset=sset, gbank=gbank, tn=tn):
                    v.wait_ge(sT.h, tn)
                    v.tensor_scalar(out=gc_v[sset], in0=pb(gbank),
                                    scalar1=bg_sb[:, e * 16 + m:e * 16 + m + 1],
                                    scalar2=7.0, op0=ALU.add,
                                    op1=ALU.min).then_inc(sV.h, 1)
                em("v", v_gc)

                sglu = sS.inc(1)
                s_glu_done[(e, idx)] = sglu

                def s_glu(s, sset=sset, vgc=vgc):
                    s.wait_ge(sV.h, vgc)
                    s.activation(out=glu_v[sset], in_=gc_v[sset], func=AF.Silu,
                                 bias=0.0, scale=ALPHA).then_inc(sS.h, 1)
                em("s", s_glu)

                vact = sV.inc(1)
                v_act_done[(e, idx)] = vact

                def v_act(v, e=e, m=m, tc=tc, sset=sset, ubank=ubank,
                          sglu=sglu, vbc=vb):
                    v.tensor_scalar(out=u3_v[sset], in0=pb(ubank),
                                    scalar1=bu1_sb[:, e * 16 + m:e * 16 + m + 1],
                                    scalar2=8.0, op0=ALU.add, op1=ALU.min)
                    v.wait_ge(sS.h, sglu)
                    v.scalar_tensor_tensor(out=a1_v[sset], in0=u3_v[sset],
                                           scalar=-6.0, in1=glu_v[sset],
                                           op0=ALU.max, op1=ALU.mult)
                    if e == 0 and m == 0 and tc == 0:
                        v.wait_ge(sV.h, vbc)
                    v.tensor_tensor(out=act_v[:, m, tc * 512:(tc + 1) * 512],
                                    in0=a1_v[sset],
                                    in1=bcast_v[e % 2][:, tc * 512:(tc + 1) * 512],
                                    op=ALU.mult).then_inc(sV.h, 1)
                em("v", v_act)
        act_all[e] = v_act_done[(e, 31)]

        for hs in range(KC):
            ph_ = hs % 2
            dw_free = pe_down_done.get((e, hs - 2)) if hs >= 2 else \
                pe_down_done.get((e - 1, 14 + hs))

            def g_dn(g, e=e, hs=hs, ph_=ph_, dw_free=dw_free, vev8=V_EV8):
                if e == 0 and hs < 2:
                    g.wait_ge(sV.h, vev8)
                if dw_free is not None:
                    g.wait_ge(sT.h, dw_free)
                g.dma_start(out=dwbuf_v[ph_][:], in_=d_dn[e, hs]).then_inc(qD[ph_].h, 16)
            em("g", g_dn)
            wd = qD[ph_].inc(16)

            bank0 = 4 if hs % 2 == 0 else 6
            bank_free = v_down_done.get((e, hs - 2)) if hs >= 2 else \
                v_down_done.get((e - 1, 14 + hs))
            if bank0 == 6:
                bank_free2 = v_bcast_done[e]
            else:
                bank_free2 = None

            tn = sT.inc(1)
            pe_down_done[(e, hs)] = tn

            def t_dn(t, e=e, hs=hs, ph_=ph_, bank0=bank0, wd=wd,
                     bank_free=bank_free, bank_free2=bank_free2, wact=act_all[e]):
                t.wait_ge(qD[ph_].h, wd)
                if hs == 0:
                    t.wait_ge(sV.h, wact)
                if bank_free is not None:
                    t.wait_ge(sV.h, bank_free)
                if bank_free2 is not None:
                    t.wait_ge(sV.h, bank_free2)
                last = None
                for tc in range(2):
                    dst = pb(bank0 + tc)
                    for k in range(KC):
                        last = t.matmul(dst, dwbuf_v[ph_][:, k, :],
                                        act_v[:, k, tc * 512:(tc + 1) * 512],
                                        start=(k == 0), stop=(k == KC - 1))
                last.then_inc(sT.h, 1)
            em("t", t_dn)

            vn = sV.inc(1)
            v_down_done[(e, hs)] = vn

            def v_dn(v, hs=hs, bank0=bank0, tn=tn):
                v.wait_ge(sT.h, tn)
                v.tensor_tensor(out=ACC[:, hs, :], in0=ACC[:, hs, :],
                                in1=p2(bank0), op=ALU.add).then_inc(sV.h, 1)
            em("v", v_dn)

    PH2_V_END = sV.n
    PH2_T_END = sT.n

    pe_wi_done = {}
    v_proc_done = {}
    v_xc_done = {}
    s_gate_done = {}
    pe_wo_done = {}
    v_sg_done = {}
    v_y_done = {}
    s_sil_done = {}

    for q in range(NQ):
        c0 = q * TQ
        for si in range(64):
            pw = si % 2
            i = si // 2
            is_xi = (si % 2 == 0)
            ds = i if is_xi else 32 + i
            wi_free = pe_wi_done.get((q, si - 2))
            if wi_free is None and q > 0:
                wi_free = pe_wi_done.get((q - 1, 62 + si))
            first_q0 = (q == 0 and si < 2)

            def g_wi(g, ds=ds, pw=pw, wi_free=wi_free, first_q0=first_q0,
                     wt=PH2_T_END):
                if first_q0:
                    g.wait_ge(sT.h, wt)
                if wi_free is not None:
                    g.wait_ge(sT.h, wi_free)
                g.dma_start(out=wibuf_v[pw][:], in_=d_wi[ds]).then_inc(qW[pw].h, 16)
            em("g", g_wi)
            ww = qW[pw].inc(16)

            bank = si % 4
            if is_xi:
                bank_free = v_xc_done.get((q, i - 1))
                bank_free = v_xc_done.get((q, i - 2))
                if bank_free is None and q > 0:
                    bank_free = v_xc_done.get((q - 1, 30 + i))
            else:
                bank_free = s_gate_done.get((q, i - 2))
                if bank_free is None and q > 0:
                    bank_free = s_gate_done.get((q - 1, 30 + i))
            first_q = (si < 4)
            y_prev = v_y_done.get(2 * q - 1) if first_q and q > 0 else None
            ph2_bar = (q == 0 and si < 4)

            tn = sT.inc(1)
            pe_wi_done[(q, si)] = tn

            def t_ip(t, q=q, si=si, i=i, is_xi=is_xi, pw=pw, bank=bank, ww=ww,
                     bank_free=bank_free, y_prev=y_prev, ph2_bar=ph2_bar,
                     c0=c0, wv=PH2_V_END):
                t.wait_ge(qW[pw].h, ww)
                if ph2_bar:
                    t.wait_ge(sV.h, wv)
                if y_prev is not None:
                    t.wait_ge(sV.h, y_prev)
                if bank_free is not None:
                    if is_xi:
                        t.wait_ge(sV.h, bank_free)
                    else:
                        t.wait_ge(sS.h, bank_free)
                last = None
                if is_xi:
                    d0 = pb(bank)[:, 0:256]
                    d1 = pb(bank)[:, 256:259]
                    for k in range(KC):
                        t.matmul(d0, wibuf_v[pw][:, k, :], xnT[:, k, c0:c0 + 256],
                                 start=(k == 0), stop=False,
                                 skip_group_check=True)
                        last = t.matmul(d1, wibuf_v[pw][:, k, :],
                                        xnT[:, k, c0 + 256:c0 + 259],
                                        start=False, stop=(k == KC - 1),
                                        skip_group_check=True)
                else:
                    d0 = pb(bank)[:, 0:256]
                    for k in range(KC):
                        last = t.matmul(d0, wibuf_v[pw][:, k, :],
                                        xnT[:, k, c0 + 3:c0 + 259],
                                        start=(k == 0), stop=(k == KC - 1))
                last.then_inc(sT.h, 1)
            em("t", t_ip)

            if is_xi:
                xibank = bank
                vxc = sV.inc(1)
                v_xc_done[(q, i)] = vxc
                s2 = i % 2
                tset_free = v_proc_done.get((q, i - 2))
                if tset_free is None and q > 0:
                    tset_free = v_proc_done.get((q - 1, 30 + i))

                def v_cv(v, q=q, i=i, s2=s2, xibank=xibank, tn=tn,
                         tset_free=tset_free):
                    v.wait_ge(sT.h, tn)
                    if tset_free is not None:
                        v.wait_ge(sV.h, tset_free)
                    ps = pb(xibank)
                    cw4 = i * 4
                    v.tensor_scalar(out=cv0_v[s2], in0=ps[:, 0:256],
                                    scalar1=cw_sb[:, cw4:cw4 + 1], scalar2=None,
                                    op0=ALU.mult)
                    v.scalar_tensor_tensor(out=cv1_v[s2], in0=ps[:, 1:257],
                                           scalar=cw_sb[:, cw4 + 1:cw4 + 2],
                                           in1=cv0_v[s2], op0=ALU.mult, op1=ALU.add)
                    v.scalar_tensor_tensor(out=cv0_v[s2], in0=ps[:, 2:258],
                                           scalar=cw_sb[:, cw4 + 2:cw4 + 3],
                                           in1=cv1_v[s2], op0=ALU.mult, op1=ALU.add)
                    v.scalar_tensor_tensor(out=cv1_v[s2], in0=ps[:, 3:259],
                                           scalar=cw_sb[:, cw4 + 3:cw4 + 4],
                                           in1=cv0_v[s2], op0=ALU.mult,
                                           op1=ALU.add).then_inc(sV.h, 1)
                em("v", v_cv)

                ssx = sS.inc(1)

                def s_sx(s, i=i, s2=s2, vxc=vxc):
                    s.wait_ge(sV.h, vxc)
                    s.activation(out=sx_v[s2], in_=cv1_v[s2], func=AF.Silu,
                                 bias=cb_sb[:, i:i + 1], scale=1.0).then_inc(sS.h, 1)
                em("s", s_sx)
                v_cv.ssx = ssx
                pe_wi_done[("ssx", q, i)] = ssx
            else:
                gbank_ = bank
                sgt = sS.inc(1)
                s_gate_done[(q, i)] = sgt

                def s_sg(s, gbank_=gbank_, s2=i % 2, tn=tn):
                    s.wait_ge(sT.h, tn)
                    s.activation(out=sgx_v[s2], in_=pb(gbank_)[:, 0:256],
                                 func=AF.Silu, bias=0.0, scale=1.0).then_inc(sS.h, 1)
                em("s", s_sg)

                vh = sV.inc(1)
                v_proc_done[(q, i)] = vh
                ssx = pe_wi_done[("ssx", q, i)]

                def v_h(v, q=q, i=i, s2=i % 2, ssx=ssx, sgt=sgt):
                    v.wait_ge(sS.h, max(ssx, sgt))
                    v.scalar_tensor_tensor(out=h_v[:, i, :], in0=sgx_v[s2],
                                           scalar=dv_sb[:, i:i + 1], in1=sx_v[s2],
                                           op0=ALU.mult,
                                           op1=ALU.mult).then_inc(sV.h, 1)
                em("v", v_h)

        H_ALL = v_proc_done[(q, 31)]

        for cs in range(KC):
            po = cs % 2
            wo_free = pe_wo_done.get((q, cs - 2))
            if wo_free is None and q > 0:
                wo_free = pe_wo_done.get((q - 1, 14 + cs))
            first_q0 = (q == 0 and cs < 2)

            def g_wo(g, cs=cs, po=po, wo_free=wo_free, first_q0=first_q0,
                     wt=PH2_T_END):
                if first_q0:
                    g.wait_ge(sT.h, wt)
                if wo_free is not None:
                    g.wait_ge(sT.h, wo_free)
                g.dma_start(out=wobuf_v[po][:], in_=d_wo[cs]).then_inc(qO[po].h, 16)
            em("g", g_wo)
            wo = qO[po].inc(16)

            sbank = 4 + cs % 2
            sb_free = v_sg_done.get((q, cs - 2))
            if sb_free is None and q > 0:
                sb_free = v_sg_done.get((q - 1, 14 + cs))
            if sb_free is None:
                sb_free = PH2_V_END

            tn = sT.inc(1)
            pe_wo_done[(q, cs)] = tn

            def t_op(t, q=q, cs=cs, po=po, sbank=sbank, wo=wo, sb_free=sb_free,
                     h_all=H_ALL):
                t.wait_ge(qO[po].h, wo)
                if cs == 0:
                    t.wait_ge(sV.h, h_all)
                t.wait_ge(sV.h, sb_free)
                dst = pb(sbank)[:, 0:256]
                last = None
                for k in range(32):
                    last = t.matmul(dst, wobuf_v[po][:, k, :], h_v[:, k, :],
                                    start=(k == 0), stop=(k == 31))
                last.then_inc(sT.h, 1)
            em("t", t_op)

            ssil = sS.inc(1)
            s_sil_done[(q, cs)] = ssil
            sil_free = v_sg_done.get((q, cs - 2))
            if sil_free is None and q > 0:
                sil_free = v_sg_done.get((q - 1, 14 + cs))

            def s_sil(s, q=q, cs=cs, w=PH2_V_END, sil_free=sil_free):
                if q == 0 and cs == 0:
                    s.wait_ge(sV.h, w)
                if sil_free is not None:
                    s.wait_ge(sV.h, sil_free)
                s.activation(out=sil_v[cs % 2], in_=ACC[:, cs, q * TQ:(q + 1) * TQ],
                             func=AF.Silu, bias=0.0, scale=1.0).then_inc(sS.h, 1)
            em("s", s_sil)

            vsg = sV.inc(1)
            v_sg_done[(q, cs)] = vsg

            def v_sg(v, q=q, cs=cs, sbank=sbank, tn=tn, ssil=ssil):
                v.wait_ge(sT.h, tn)
                v.wait_ge(sS.h, ssil)
                v.tensor_tensor(out=sg_v[:, cs, :], in0=pb(sbank)[:, 0:256],
                                in1=sil_v[cs % 2], op=ALU.mult).then_inc(sV.h, 1)
            em("v", v_sg)

        SG_ALL = v_sg_done[(q, 15)]

        for tb in range(2):
            tb_g = 2 * q + tb
            pr = tb_g % 2

            def q_xr(q_, tb_g=tb_g, pr=pr, w=PH2_V_END):
                if tb_g < 2:
                    q_.wait_ge(sV.h, w)
                if tb_g >= 2:
                    q_.wait_ge(qY.h, 16 * (tb_g - 1))
                q_.dma_start(out=xres_v[pr][:],
                             in_=d_xh[3 + tb_g * P:3 + tb_g * P + P, :]
                             ).then_inc(qR[pr].h, 16)
            em("q", q_xr)
            wr = qR[pr].inc(16)

            y_free = v_y_done.get(tb_g - 1)
            tn = sT.inc(1)

            def t_y(t, q=q, tb=tb, sg_all=SG_ALL, y_free=y_free):
                t.wait_ge(sV.h, sg_all)
                if y_free is not None:
                    t.wait_ge(sV.h, y_free)
                last = None
                for cs in range(KC):
                    last = t.transpose(PSA[:, cs * P:(cs + 1) * P],
                                       sg_v[:, cs, tb * P:(tb + 1) * P], identf[:])
                last.then_inc(sT.h, 1)
            em("t", t_y)

            vy = sV.inc(1)
            v_y_done[tb_g] = vy

            def v_y(v, tb_g=tb_g, pr=pr, tn=tn, wr=wr):
                v.wait_ge(sT.h, tn)
                v.wait_ge(qR[pr].h, wr)
                if tb_g > 0:
                    v.wait_ge(qY.h, 16 * tb_g)
                v.tensor_tensor(out=y_v[:], in0=PSA[:, 0:2048], in1=xres_v[pr][:],
                                op=ALU.add).then_inc(sV.h, 1)
            em("v", v_y)

            def q_y(q_, tb_g=tb_g, vy=vy):
                q_.wait_ge(sV.h, vy)
                q_.dma_start(out=d_y[tb_g * P:(tb_g + 1) * P, :],
                             in_=y_v[:]).then_inc(qY.h, 16)
            em("q", q_y)
            qY.inc(16)

    Y_ALL = qY.n

    def q_fin(q_, w=Y_ALL):
        q_.wait_ge(qY.h, w)
    em("q", q_fin)

    class _InstrProxy:
        def __init__(self, ae, instr):
            self._ae = ae
            self._instr = instr

        def then_inc(self, sem, k):
            self._instr.then_inc(sem, k)
            self._ae.note_explicit(self, sem, k)
            return self

    class AutoEdge:

        def __init__(self, eng, auto_sem):
            self._e = eng
            self._auto = auto_sem
            self._auto_n = 0
            self._cnt = {}
            self._pend = None
            self._tok = None

        def note_explicit(self, proxy, sem, k):
            if self._pend is not None and self._pend._instr is proxy._instr:
                c = self._cnt.get(id(sem), 0) + k
                self._cnt[id(sem)] = c
                self._tok = (sem, c)
                self._pend = None

        def _barrier(self):
            if self._pend is not None:
                self._auto_n += 1
                self._pend._instr.then_inc(self._auto, 1)
                self._tok = (self._auto, self._auto_n)
                self._pend = None
            if self._tok is not None:
                self._e.wait_ge(*self._tok)
                self._tok = None

        def wait_ge(self, sem, val):
            return self._e.wait_ge(sem, val)

        def __getattr__(self, name):
            real = getattr(self._e, name)

            def f(*a, **kw):
                self._barrier()
                instr = real(*a, **kw)
                p = _InstrProxy(self, instr)
                self._pend = p
                return p
            return f

    with nc.Block() as block:
        @block.tensor
        def _(t):
            for f in OPS["t"]:
                f(t)

        @block.vector
        def _(v):
            ae = AutoEdge(v, sVa.h)
            for f in OPS["v"]:
                f(ae)

        @block.scalar
        def _(s):
            ae = AutoEdge(s, sSa.h)
            for f in OPS["s"]:
                f(ae)

        @block.sync
        def _(q):
            for f in OPS["q"]:
                f(q)

        @block.gpsimd
        def _(g):
            for f in OPS["g"]:
                f(g)

    nc._keepalive = st
    return nc


_NC_CACHE = None


def _get_nc():
    global _NC_CACHE
    if _NC_CACHE is None:
        _NC_CACHE = build_program()
    return _NC_CACHE


def kernel(L0_attn_x, L0_attn_w_in, L0_attn_w_out, L0_attn_conv_w, L0_attn_conv_b,
           L0_attn_D, L0_ffn_router_weight, L0_ffn_router_bias, L0_ffn_gate_up_proj,
           L0_ffn_gate_up_proj_bias, L0_ffn_down_proj, L0_ffn_down_proj_bias):
    f32 = np.float32
    C = lambda a: np.ascontiguousarray(a, dtype=f32)

    x = np.asarray(L0_attn_x, dtype=f32).reshape(8192, D)
    w_in = np.asarray(L0_attn_w_in, dtype=f32)
    w_out = np.asarray(L0_attn_w_out, dtype=f32)
    conv_w = np.asarray(L0_attn_conv_w, dtype=f32)
    conv_b = np.asarray(L0_attn_conv_b, dtype=f32)
    Dvec = np.asarray(L0_attn_D, dtype=f32)
    rw = np.asarray(L0_ffn_router_weight, dtype=f32)
    rbias = np.asarray(L0_ffn_router_bias, dtype=f32)
    gup = np.asarray(L0_ffn_gate_up_proj, dtype=f32)
    gup_b = np.asarray(L0_ffn_gate_up_proj_bias, dtype=f32)
    down = np.asarray(L0_ffn_down_proj, dtype=f32)
    down_b = np.asarray(L0_ffn_down_proj_bias, dtype=f32)

    Wi = C(w_in.reshape(64, P, KC, P).transpose(0, 3, 2, 1))
    Wo = C(w_out.reshape(16, P, 32, P).transpose(0, 3, 2, 1))
    gate = gup[:, :, 0::2]
    up = gup[:, :, 1::2]
    G = C(gate.reshape(E, KC, P, 16, P).transpose(0, 3, 2, 1, 4))
    U = C(up.reshape(E, KC, P, 16, P).transpose(0, 3, 2, 1, 4))
    Dn = C(down.reshape(E, KC, P, 16, P).transpose(0, 3, 2, 1, 4))
    Rt = C(rw.reshape(E, KC, P).transpose(2, 1, 0))
    Rb = C(rbias.reshape(E, 1))
    Db = C(down_b)
    Bg = C(gup_b[:, 0::2].reshape(E, 16, P).transpose(2, 0, 1).reshape(P, 256))
    Bu = C(gup_b[:, 1::2].reshape(E, 16, P).transpose(2, 0, 1).reshape(P, 256))
    Cw = C(conv_w[:, 0, :].reshape(32, P, 4).transpose(1, 0, 2).reshape(P, P))
    Cb = C(conv_b.reshape(32, P).T)
    Dv = C(Dvec.reshape(32, P).T)

    xpad = np.concatenate([np.zeros((3, D), f32), x], axis=0)

    shared = dict(wi=Wi, wo=Wo, gw=G, uw=U, dw=Dn, rt=Rt, rb=Rb, db=Db,
                  bg=Bg, bu=Bu, cw=Cw, cb=Cb, dv=Dv)
    in_maps = []
    for c in range(8):
        m = dict(shared)
        m["xh"] = C(xpad[T * c:T * c + TH])
        in_maps.append(m)

    nc = _get_nc()
    res = run_bass_kernel_spmd(nc, in_maps, core_ids=list(range(8)))
    y = np.concatenate([res.results[c]["y"] for c in range(8)], axis=0)
    return y.reshape(1, 8192, D)


# version 04
# speedup vs baseline: 1.2572x; 1.2572x over previous
import numpy as np
from contextlib import ExitStack

import concourse.bass as bass
import concourse.mybir as mybir
from concourse.bass_utils import run_bass_kernel_spmd

F32 = mybir.dt.float32
BF16 = mybir.dt.bfloat16
AF = mybir.ActivationFunctionType
ALU = mybir.AluOpType
AX = mybir.AxisListType.X

P = 128
T = 1024
TH = 1027
D = 2048
KC = 16
E = 16
ALPHA = 1.702
INVA = float(np.float32(1.0) / np.float32(1.702))
EPS = 1e-5
BIG = 1.0e30
TQ = 256
NQ = 4


class Sem:
    def __init__(self, nc, st, name):
        self.h = st.enter_context(nc.semaphore(name))
        self.n = 0

    def inc(self, k=1):
        self.n += k
        return self.n


def build_program():
    nc = bass.Bass("TRN2", target_bir_lowering=False, debug=False)

    d_xh = nc.dram_tensor("xh", [TH, D], F32, kind="ExternalInput").ap()
    d_wi = nc.dram_tensor("wi", [64, P, KC, P], F32, kind="ExternalInput").ap()
    d_wo = nc.dram_tensor("wo", [16, P, 32, P], F32, kind="ExternalInput").ap()
    d_g = nc.dram_tensor("gw", [E, 16, P, KC, P], F32, kind="ExternalInput").ap()
    d_u = nc.dram_tensor("uw", [E, 16, P, KC, P], F32, kind="ExternalInput").ap()
    d_dn = nc.dram_tensor("dw", [E, 16, P, KC, P], F32, kind="ExternalInput").ap()
    d_rt = nc.dram_tensor("rt", [P, KC, E], F32, kind="ExternalInput").ap()
    d_rb = nc.dram_tensor("rb", [E, 1], F32, kind="ExternalInput").ap()
    d_db = nc.dram_tensor("db", [E, D], F32, kind="ExternalInput").ap()
    d_bg = nc.dram_tensor("bg", [P, 256], F32, kind="ExternalInput").ap()
    d_bu = nc.dram_tensor("bu", [P, 256], F32, kind="ExternalInput").ap()
    d_cw = nc.dram_tensor("cw", [P, P], F32, kind="ExternalInput").ap()
    d_cb = nc.dram_tensor("cb", [P, 32], F32, kind="ExternalInput").ap()
    d_dv = nc.dram_tensor("dv", [P, 32], F32, kind="ExternalInput").ap()
    d_y = nc.dram_tensor("y", [T, D], F32, kind="ExternalOutput").ap()

    st = ExitStack()
    e_ = st.enter_context

    xnT = e_(nc.sbuf_tensor("xnT", [P, KC, 1028], BF16))
    ACC = e_(nc.sbuf_tensor("ACC", [P, KC, T], F32))
    BSCR = e_(nc.sbuf_tensor("BSCR", [P, 16384], BF16))
    STRM = e_(nc.sbuf_tensor("STRM", [P, 20480], BF16))
    TMPP = e_(nc.sbuf_tensor("TMPP", [P, 10240], BF16))

    identb = e_(nc.sbuf_tensor("identb", [P, P], BF16))
    identf = e_(nc.sbuf_tensor("identf", [P, P], F32))
    rt_sb = e_(nc.sbuf_tensor("rt_sb", [P, KC, E], BF16))
    rb_sb = e_(nc.sbuf_tensor("rb_sb", [E, 1], F32))
    db_sb = e_(nc.sbuf_tensor("db_sb", [E, D], BF16))
    bg_sb = e_(nc.sbuf_tensor("bg_sb", [P, 256], F32))
    bu1_sb = e_(nc.sbuf_tensor("bu1_sb", [P, 256], F32))
    cw_sb = e_(nc.sbuf_tensor("cw_sb", [P, P], F32))
    cb_sb = e_(nc.sbuf_tensor("cb_sb", [P, 32], F32))
    dv_sb = e_(nc.sbuf_tensor("dv_sb", [P, 32], F32))
    sel_sb = e_(nc.sbuf_tensor("sel_sb", [E, 16 * P], BF16))
    eps_sb = e_(nc.sbuf_tensor("eps_sb", [P, 1], F32))
    ln_sc = e_(nc.sbuf_tensor("ln_sc", [P, 8], F32))
    lg_tm = e_(nc.sbuf_tensor("lg_tm", [P, 8, E], F32))
    sc_tm = e_(nc.sbuf_tensor("sc_tm", [P, 8, E], F32))
    tkw = e_(nc.sbuf_tensor("tkw", [P, E], F32))
    tkm = e_(nc.sbuf_tensor("tkm", [P, 8], F32))
    tke = e_(nc.sbuf_tensor("tke", [P, E], F32))
    tke2 = e_(nc.sbuf_tensor("tke2", [P, E], F32))
    scTb = e_(nc.sbuf_tensor("scTb", [E, T], BF16))
    sel_v = sel_sb[:].rearrange("p (a b) -> p a b", a=16)

    PSA = e_(nc.psum_tensor("PSA", [P, 2048], F32))
    PSB = e_(nc.psum_tensor("PSB", [P, 2048], F32))

    def pb(i):
        t = PSA if i < 4 else PSB
        j = i % 4
        return t[:, j * 512:(j + 1) * 512]

    def pbb(i):
        t = (PSA if i < 4 else PSB)[:].bitcast(BF16)
        j = i % 4
        return t[:, j * 1024:(j + 1) * 1024]

    def p2(i):
        t = PSA if i < 4 else PSB
        j = i % 4
        return t[:, j * 512:(j + 2) * 512]

    def carve(base, byte_off, dtype, n0, n1=None):
        esz = 4 if dtype == F32 else 2
        nel = n0 * (n1 or 1)
        v = base[:, byte_off // 2:(byte_off + nel * esz) // 2]
        if dtype == F32:
            v = v.bitcast(F32)
        if n1 is not None:
            v = v.rearrange("p (a b) -> p a b", a=n0)
        return v

    act_v = carve(BSCR, 0, BF16, 16, 1024)
    h_v = carve(BSCR, 0, BF16, 32, 512)
    ACCF = ACC[:].rearrange("p a b -> p (a b)")
    g_v = ACCF[:, 0:8192].bitcast(BF16).rearrange("p (a b) -> p a b", a=16)
    sg_v = ACCF[:, 8192:16384].rearrange("p (a b) -> p a b", a=16)

    xt_v = [carve(STRM, 0, F32, 2048), carve(STRM, 8192, F32, 2048)]
    xn_v = [carve(STRM, 16384, BF16, 2048), carve(STRM, 20480, BF16, 2048)]
    gbuf_v = [carve(STRM, 0, BF16, 16, 128), carve(STRM, 4096, BF16, 16, 128)]
    ubuf_v = [carve(STRM, 8192, BF16, 16, 128), carve(STRM, 12288, BF16, 16, 128)]
    dwbuf_v = [carve(STRM, 16384, BF16, 16, 128), carve(STRM, 20480, BF16, 16, 128)]
    bcast_v = [carve(STRM, 24576, F32, 1024), carve(STRM, 28672, F32, 1024)]
    wibuf_v = [carve(STRM, j * 4096, BF16, 16, 128) for j in range(4)]
    wobuf_v = [carve(STRM, 16384, BF16, 32, 128), carve(STRM, 24576, BF16, 32, 128)]
    xres_v = carve(STRM, 32768, F32, 2048)

    xcen_v = carve(TMPP, 0, F32, 2048)
    sq_v = carve(TMPP, 8192, F32, 2048)
    gc_v = [carve(TMPP, 0, F32, 512), carve(TMPP, 8192, F32, 512)]
    glu_v = [carve(TMPP, 2048, F32, 512), carve(TMPP, 10240, F32, 512)]
    u3_v = [carve(TMPP, 4096, F32, 512), carve(TMPP, 12288, F32, 512)]
    a1_v = [carve(TMPP, 6144, F32, 512), carve(TMPP, 14336, F32, 512)]
    cv0_v = carve(TMPP, 0, F32, 512)
    cv1_v = carve(TMPP, 2048, F32, 512)
    sx_v = carve(TMPP, 4096, F32, 512)
    sgx_v = carve(TMPP, 6144, F32, 512)
    y_v = carve(TMPP, 8192, F32, 2048)
    g0scr = carve(TMPP, 8192, BF16, 1024)
    lgT_sb = carve(TMPP, 0, F32, 1024)[0:16, :]

    qP = Sem(nc, st, "qP")
    qPg = Sem(nc, st, "qPg")
    qX = [Sem(nc, st, "qXa"), Sem(nc, st, "qXb")]
    qG = [Sem(nc, st, "qGa"), Sem(nc, st, "qGb")]
    qU = [Sem(nc, st, "qUa"), Sem(nc, st, "qUb")]
    qD = [Sem(nc, st, "qDa"), Sem(nc, st, "qDb")]
    qW = [Sem(nc, st, "qWa"), Sem(nc, st, "qWb")]
    qO = [Sem(nc, st, "qOa"), Sem(nc, st, "qOb")]
    qR = [Sem(nc, st, "qRa"), Sem(nc, st, "qRb")]
    qY = Sem(nc, st, "qY")
    sT = Sem(nc, st, "sT")
    sV = Sem(nc, st, "sV")
    sS = Sem(nc, st, "sS")
    sG = Sem(nc, st, "sG")
    sVa = Sem(nc, st, "sVa")
    sSa = Sem(nc, st, "sSa")

    OPS = {k: [] for k in ("t", "v", "s", "q", "g")}

    def em(k, fn):
        OPS[k].append(fn)

    def g0(g):
        g.memset(identb[:], 0.0).then_inc(sG.h, 1)
        g.memset(identf[:], 0.0).then_inc(sG.h, 1)
        g.memset(sel_sb[:], 0.0).then_inc(sG.h, 1)
        g.memset(eps_sb[:], EPS).then_inc(sG.h, 1)
        g.wait_ge(sG.h, 4)
        g.affine_select(out=identb[:], in_=identb[:], compare_op=ALU.not_equal,
                        fill=1.0, base=0, pattern=[[-1, P]],
                        channel_multiplier=1).then_inc(sG.h, 1)
        g.affine_select(out=identf[:], in_=identf[:], compare_op=ALU.not_equal,
                        fill=1.0, base=0, pattern=[[-1, P]],
                        channel_multiplier=1).then_inc(sG.h, 1)
        g.affine_select(out=sel_v[:, :, :], in_=sel_v[:, :, :],
                        compare_op=ALU.not_equal, fill=1.0, base=0,
                        pattern=[[-1, 16], [0, P]],
                        channel_multiplier=1).then_inc(sG.h, 1)
        g.dma_start(out=rt_sb[:], in_=d_rt).then_inc(qPg.h, 16)
        g.dma_start(out=db_sb[:], in_=d_db).then_inc(qPg.h, 16)
    em("g", g0)
    sG.inc(6)
    GCONST = sG.inc(1)
    qPg.inc(32)

    def q0(q):
        q.dma_start(out=rb_sb[:], in_=d_rb).then_inc(qP.h, 16)
        q.dma_start(out=bg_sb[:], in_=d_bg).then_inc(qP.h, 16)
        q.dma_start(out=bu1_sb[:], in_=d_bu).then_inc(qP.h, 16)
        q.dma_start(out=cw_sb[:], in_=d_cw).then_inc(qP.h, 16)
        q.dma_start(out=cb_sb[:], in_=d_cb).then_inc(qP.h, 16)
        q.dma_start(out=dv_sb[:], in_=d_dv).then_inc(qP.h, 16)
    em("q", q0)
    qP.inc(96)
    P_ALL = qP.n
    PG_ALL = qPg.n

    def v0(v, w=P_ALL):
        v.wait_ge(qP.h, w)
        v.tensor_scalar(out=bu1_sb[:], in0=bu1_sb[:], scalar1=1.0, scalar2=None,
                        op0=ALU.add)
    em("v", v0)

    nrows = [P] * 8 + [3]
    v_xn = {}
    v_ev = {}
    t_tr = {}
    for j in range(9):
        par = j % 2
        r = nrows[j]

        free = v_xn.get(j - 2)

        def qld(q, j=j, par=par, r=r, free=free):
            if free is not None:
                q.wait_ge(sV.h, free)
            q.dma_start(out=xt_v[par][0:r, :], in_=d_xh[j * P:j * P + r, :]
                        ).then_inc(qX[par].h, 16)
        em("q", qld)
        wx = qX[par].inc(16)

        a1 = sV.inc(1)
        a2 = sV.inc(1)

        def vln(v, par=par, r=r, wx=wx, a1=a1):
            v.wait_ge(qX[par].h, wx)
            xt = xt_v[par][0:r, :]
            v.reduce_sum(ln_sc[0:r, 0:1], xt, axis=AX).then_inc(sV.h, 1)
            v.wait_ge(sV.h, a1)
            v.tensor_scalar(out=ln_sc[0:r, 1:2], in0=ln_sc[0:r, 0:1],
                            scalar1=1.0 / D, scalar2=None, op0=ALU.mult)
            v.tensor_scalar(out=xcen_v[0:r, :], in0=xt, scalar1=ln_sc[0:r, 1:2],
                            scalar2=None, op0=ALU.subtract)
            v.tensor_tensor(out=sq_v[0:r, :], in0=xcen_v[0:r, :],
                            in1=xcen_v[0:r, :], op=ALU.mult)
            v.reduce_sum(ln_sc[0:r, 2:3], sq_v[0:r, :], axis=AX).then_inc(sV.h, 1)
        em("v", vln)

        b1 = sS.inc(1)
        b2 = sS.inc(1)

        def sln(s, r=r, a2=a2, b1=b1, j=j):
            if j == 0:
                s.wait_ge(sG.h, GCONST)
                s.wait_ge(qP.h, P_ALL)
            s.wait_ge(sV.h, a2)
            s.activation(out=ln_sc[0:r, 3:4], in_=ln_sc[0:r, 2:3], func=AF.Ln,
                         bias=eps_sb[0:r, :], scale=1.0 / D).then_inc(sS.h, 1)
            s.wait_ge(sS.h, b1)
            s.activation(out=ln_sc[0:r, 4:5], in_=ln_sc[0:r, 3:4], func=AF.Exp,
                         bias=0.0, scale=-0.5).then_inc(sS.h, 1)
        em("s", sln)

        c1 = sV.inc(1)
        v_xn[j] = c1
        ttr_prev = t_tr.get(j - 2)

        def vxn(v, par=par, r=r, b2=b2, ttr_prev=ttr_prev):
            v.wait_ge(sS.h, b2)
            if ttr_prev is not None:
                v.wait_ge(sT.h, ttr_prev)
            v.tensor_scalar(out=xn_v[par][0:r, :], in0=xcen_v[0:r, :],
                            scalar1=ln_sc[0:r, 4:5], scalar2=None,
                            op0=ALU.mult).then_inc(sV.h, 1)
        em("v", vxn)

        d1 = sT.inc(1)
        t_tr[j] = d1
        ev_prev = v_ev.get(j - 1)

        def ttr(t, par=par, r=r, c1=c1, ev_prev=ev_prev):
            t.wait_ge(sV.h, c1)
            t.wait_ge(sG.h, GCONST)
            if ev_prev is not None:
                t.wait_ge(sV.h, ev_prev)
            xn3 = xn_v[par].rearrange("p (a b) -> p a b", a=KC)
            blk = r if r % 2 == 0 else r + 1
            last = None
            for k in range(KC):
                bank = 0 if k < 8 else 1
                dst = pbb(bank)[:, (k % 8) * blk:(k % 8) * blk + r]
                last = t.transpose(dst, xn3[0:r, k, :], identb[0:r, 0:r])
            last.then_inc(sT.h, 1)
        em("t", ttr)

        e1 = sV.inc(1)
        v_ev[j] = e1

        def vev(v, j=j, r=r, d1=d1):
            v.wait_ge(sT.h, d1)
            blk = r if r % 2 == 0 else r + 1
            for bank in range(2):
                src = pbb(bank)[:, 0:8 * blk].rearrange("p (a b) -> p a b", a=8)
                dst = xnT[:, bank * 8:bank * 8 + 8, j * P:j * P + r]
                v.tensor_copy(dst, src[:, :, 0:r])
            v.nop().then_inc(sV.h, 1)
        em("v", vev)

    V_XN8 = v_xn[8]
    V_EV8 = v_ev[8]

    rT = sT.inc(1)

    def t_lg(t, ev8=V_EV8, pall=PG_ALL):
        t.wait_ge(sV.h, ev8)
        t.wait_ge(qPg.h, pall)
        last = None
        for c2 in range(2):
            dst = pb(2 + c2)[0:16, :]
            for k in range(KC):
                last = t.matmul(dst, rt_sb[:, k, :],
                                xnT[:, k, 3 + c2 * 512:3 + c2 * 512 + 512],
                                start=(k == 0), stop=(k == KC - 1))
        last.then_inc(sT.h, 1)
    em("t", t_lg)

    rV = sV.inc(1)

    def v_lg(v, rT=rT):
        v.wait_ge(sT.h, rT)
        v.tensor_scalar(out=lgT_sb[0:16, :], in0=PSA[0:16, 1024:2048],
                        scalar1=rb_sb[0:16, :], scalar2=None,
                        op0=ALU.add).then_inc(sV.h, 1)
    em("v", v_lg)

    rT2 = sT.inc(1)

    def t_lgt(t, rV=rV):
        t.wait_ge(sV.h, rV)
        last = None
        for j8 in range(8):
            last = t.transpose(pb(4)[:, j8 * 16:(j8 + 1) * 16],
                               lgT_sb[0:16, j8 * P:(j8 + 1) * P], identf[0:16, 0:16])
        last.then_inc(sT.h, 1)
    em("t", t_lgt)

    rV2 = sV.inc(1)

    def v_lgt(v, rT2=rT2):
        v.wait_ge(sT.h, rT2)
        v.tensor_copy(lg_tm[:, :, :], pb(4)[:, 0:P].rearrange("p (a b) -> p a b", a=8))
        v.nop().then_inc(sV.h, 1)
    em("v", v_lgt)

    sc_done_v = None
    for j8 in range(8):
        edges = [sV.inc(1) for _ in range(4)]
        mdone = edges[3]

        def v_tk1(v, j8=j8, rV2=rV2, edges=edges):
            if j8 == 0:
                v.wait_ge(sV.h, rV2)
            v.tensor_copy(tkw[:], lg_tm[:, j8, :])
            v.reduce_max(tkm[:, 4:5], tkw[:], axis=AX, negate=True).then_inc(sV.h, 1)
            v.wait_ge(sV.h, edges[0])
            v.tensor_scalar(out=tkm[:, 0:1], in0=tkm[:, 4:5], scalar1=-1.0,
                            scalar2=None, op0=ALU.mult)
            for i in range(1, 4):
                v.tensor_scalar(out=tke[:], in0=tkw[:], scalar1=tkm[:, i - 1:i],
                                scalar2=None, op0=ALU.is_ge)
                v.scalar_tensor_tensor(out=tkw[:], in0=tke[:], scalar=-BIG,
                                       in1=tkw[:], op0=ALU.mult, op1=ALU.add)
                v.reduce_max(tkm[:, i:i + 1], tkw[:], axis=AX).then_inc(sV.h, 1)
                v.wait_ge(sV.h, edges[i])
            v.tensor_scalar(out=tke[:], in0=lg_tm[:, j8, :], scalar1=tkm[:, 3:4],
                            scalar2=None, op0=ALU.is_ge)
            v.nop().then_inc(sV.h, 1)
        mask_done = sV.inc(1)
        em("v", v_tk1)

        exp_done = sS.inc(1)

        def s_tk(s, j8=j8, mask_done=mask_done):
            s.wait_ge(sV.h, mask_done)
            s.activation(out=tke2[:], in_=lg_tm[:, j8, :], func=AF.Exp,
                         bias=tkm[:, 4:5], scale=1.0).then_inc(sS.h, 1)
        em("s", s_tk)

        eden = sV.inc(1)
        erden = sV.inc(1)
        sc_done_v = sV.inc(1)

        def v_tk2(v, j8=j8, exp_done=exp_done, eden=eden, erden=erden):
            v.wait_ge(sS.h, exp_done)
            v.tensor_tensor(out=tke2[:], in0=tke2[:], in1=tke[:], op=ALU.mult)
            v.reduce_sum(tkm[:, 5:6], tke2[:], axis=AX).then_inc(sV.h, 1)
            v.wait_ge(sV.h, eden)
            v.reciprocal(tkm[:, 6:7], tkm[:, 5:6]).then_inc(sV.h, 1)
            v.wait_ge(sV.h, erden)
            v.tensor_scalar(out=sc_tm[:, j8, :], in0=tke2[:], scalar1=tkm[:, 6:7],
                            scalar2=None, op0=ALU.mult).then_inc(sV.h, 1)
        em("v", v_tk2)

    SC_TM_DONE = sc_done_v

    rT3 = sT.inc(1)

    def t_sc(t, w=SC_TM_DONE):
        t.wait_ge(sV.h, w)
        last = None
        for j8 in range(8):
            bank = 5 if j8 < 4 else 6
            last = t.transpose(pb(bank)[0:16, (j8 % 4) * 128:(j8 % 4) * 128 + 128],
                               sc_tm[:, j8, :], identf[:])
        last.then_inc(sT.h, 1)
    em("t", t_sc)

    SCT_DONE = sV.inc(1)

    def v_sc(v, rT3=rT3):
        v.wait_ge(sT.h, rT3)
        v.tensor_copy(scTb[0:16, 0:512], pb(5)[0:16, :])
        v.tensor_copy(scTb[0:16, 512:1024], pb(6)[0:16, :])
        v.nop().then_inc(sV.h, 1)
    em("v", v_sc)

    acc_init_t = {}
    acc_init_v = {}
    for hs in range(KC):
        bank0 = (hs % 2) * 2
        tn = sT.inc(1)
        acc_init_t[hs] = tn
        prev_ev = acc_init_v.get(hs - 2)

        def t_ai(t, hs=hs, bank0=bank0, prev_ev=prev_ev, w=SCT_DONE):
            if hs == 0:
                t.wait_ge(sV.h, w)
            if prev_ev is not None:
                t.wait_ge(sV.h, prev_ev)
            last = None
            for c2 in range(2):
                last = t.matmul(pb(bank0 + c2),
                                db_sb[0:16, hs * P:(hs + 1) * P],
                                scTb[0:16, c2 * 512:(c2 + 1) * 512],
                                start=True, stop=True)
            last.then_inc(sT.h, 1)
        em("t", t_ai)

        vn = sV.inc(1)
        acc_init_v[hs] = vn

        def v_ai(v, hs=hs, bank0=bank0, tn=tn):
            v.wait_ge(sT.h, tn)
            v.tensor_copy(ACC[:, hs, :], p2(bank0))
            v.nop().then_inc(sV.h, 1)
        em("v", v_ai)
    ACC_INIT_V = acc_init_v[15]
    ACC_INIT_V14 = acc_init_v[14]

    pe_slice_done = {}
    v_act_done = {}
    v_down_done = {}
    pe_down_done = {}
    v_bcast_done = {}
    act_all = {}

    for e in range(E):
        tb = sT.inc(1)
        w_prev = v_act_done.get((e - 1, 15))

        def t_bc(t, e=e, w_prev=w_prev, wsc=SCT_DONE):
            if e == 0:
                t.wait_ge(sV.h, wsc)
            if w_prev is not None:
                t.wait_ge(sV.h, w_prev)
            last = None
            for c2 in range(2):
                last = t.matmul(pb(6 + c2), sel_v[:, e, :],
                                scTb[0:16, c2 * 512:(c2 + 1) * 512],
                                start=True, stop=True)
            last.then_inc(sT.h, 1)
        em("t", t_bc)

        vb = sV.inc(1)
        v_bcast_done[e] = vb

        def v_bc(v, e=e, tb=tb):
            v.wait_ge(sT.h, tb)
            v.tensor_scalar(out=bcast_v[e % 2], in0=PSB[:, 1024:2048],
                            scalar1=INVA, scalar2=None, op0=ALU.mult)
            v.nop().then_inc(sV.h, 1)
        em("v", v_bc)

        for m in range(KC):
            pm = m % 2
            gate_free = pe_slice_done.get((e, m - 2)) if m >= 2 else \
                pe_slice_done.get((e - 1, 14 + m))

            def g_gu(g, e=e, m=m, pm=pm, gate_free=gate_free, vxn8=V_XN8):
                if e == 0 and m < 2:
                    g.wait_ge(sV.h, vxn8)
                if gate_free is not None:
                    g.wait_ge(sT.h, gate_free)
                g.dma_start(out=gbuf_v[pm][:], in_=d_g[e, m]).then_inc(qG[pm].h, 16)
                g.dma_start(out=ubuf_v[pm][:], in_=d_u[e, m]).then_inc(qU[pm].h, 16)
            em("g", g_gu)
            wg = qG[pm].inc(16)
            wu = qU[pm].inc(16)

            base = 4 * (m % 2)
            if e == 0 and m == 0:
                set_free = ACC_INIT_V
            elif m == 0:
                set_free = v_down_done[(e - 1, 15)]
            elif m == 1:
                set_free = vb
            else:
                set_free = v_act_done[(e, m - 2)]

            tn = sT.inc(1)
            pe_slice_done[(e, m)] = tn

            def t_gu(t, e=e, m=m, pm=pm, base=base, wg=wg, wu=wu,
                     set_free=set_free):
                t.wait_ge(qG[pm].h, wg)
                t.wait_ge(qU[pm].h, wu)
                t.wait_ge(sV.h, set_free)
                last = None
                for k in range(KC):
                    gw = gbuf_v[pm][:, k, :]
                    uw = ubuf_v[pm][:, k, :]
                    t.matmul(pb(base + 0), gw, xnT[:, k, 3:515],
                             start=(k == 0), stop=(k == KC - 1))
                    t.matmul(pb(base + 1), gw, xnT[:, k, 515:1027],
                             start=(k == 0), stop=(k == KC - 1))
                    t.matmul(pb(base + 2), uw, xnT[:, k, 3:515],
                             start=(k == 0), stop=(k == KC - 1))
                    last = t.matmul(pb(base + 3), uw, xnT[:, k, 515:1027],
                                    start=(k == 0), stop=(k == KC - 1))
                last.then_inc(sT.h, 1)
            em("t", t_gu)

            for tc in range(2):
                sset = tc
                gbank = base + tc
                ubank = base + 2 + tc

                vgc = sV.inc(1)

                def v_gc(v, e=e, m=m, sset=sset, gbank=gbank, tn=tn, tc=tc):
                    if tc == 0:
                        v.wait_ge(sT.h, tn)
                    v.tensor_scalar(out=gc_v[sset], in0=pb(gbank),
                                    scalar1=bg_sb[:, e * 16 + m:e * 16 + m + 1],
                                    scalar2=7.0, op0=ALU.add,
                                    op1=ALU.min).then_inc(sV.h, 1)
                em("v", v_gc)

                sglu = sS.inc(1)

                def s_glu(s, sset=sset, vgc=vgc):
                    s.wait_ge(sV.h, vgc)
                    s.activation(out=glu_v[sset], in_=gc_v[sset], func=AF.Silu,
                                 bias=0.0, scale=ALPHA).then_inc(sS.h, 1)
                em("s", s_glu)

                vact = sV.inc(1)
                if tc == 1:
                    v_act_done[(e, m)] = vact

                def v_act(v, e=e, m=m, tc=tc, sset=sset, ubank=ubank,
                          sglu=sglu, vbc=vb):
                    v.tensor_scalar(out=u3_v[sset], in0=pb(ubank),
                                    scalar1=bu1_sb[:, e * 16 + m:e * 16 + m + 1],
                                    scalar2=8.0, op0=ALU.add, op1=ALU.min)
                    v.wait_ge(sS.h, sglu)
                    v.scalar_tensor_tensor(out=a1_v[sset], in0=u3_v[sset],
                                           scalar=-6.0, in1=glu_v[sset],
                                           op0=ALU.max, op1=ALU.mult)
                    if e == 0 and m == 0 and tc == 0:
                        v.wait_ge(sV.h, vbc)
                    v.tensor_tensor(out=act_v[:, m, tc * 512:(tc + 1) * 512],
                                    in0=a1_v[sset],
                                    in1=bcast_v[e % 2][:, tc * 512:(tc + 1) * 512],
                                    op=ALU.mult).then_inc(sV.h, 1)
                em("v", v_act)
        act_all[e] = v_act_done[(e, 15)]

        for hs in range(KC):
            ph_ = hs % 2
            dw_free = pe_down_done.get((e, hs - 2)) if hs >= 2 else \
                pe_down_done.get((e - 1, 14 + hs))

            def g_dn(g, e=e, hs=hs, ph_=ph_, dw_free=dw_free, vev8=V_EV8):
                if e == 0 and hs < 2:
                    g.wait_ge(sV.h, vev8)
                if dw_free is not None:
                    g.wait_ge(sT.h, dw_free)
                g.dma_start(out=dwbuf_v[ph_][:], in_=d_dn[e, hs]).then_inc(qD[ph_].h, 16)
            em("g", g_dn)
            wd = qD[ph_].inc(16)

            bank0 = 0 if hs % 2 == 0 else 2
            bank_free = v_down_done.get((e, hs - 2))

            tn = sT.inc(1)
            pe_down_done[(e, hs)] = tn

            def t_dn(t, e=e, hs=hs, ph_=ph_, bank0=bank0, wd=wd,
                     bank_free=bank_free, wact=act_all[e]):
                t.wait_ge(qD[ph_].h, wd)
                if hs == 0:
                    t.wait_ge(sV.h, wact)
                if bank_free is not None:
                    t.wait_ge(sV.h, bank_free)
                last = None
                for k in range(KC):
                    dw = dwbuf_v[ph_][:, k, :]
                    t.matmul(pb(bank0), dw, act_v[:, k, 0:512],
                             start=(k == 0), stop=(k == KC - 1))
                    last = t.matmul(pb(bank0 + 1), dw, act_v[:, k, 512:1024],
                                    start=(k == 0), stop=(k == KC - 1))
                last.then_inc(sT.h, 1)
            em("t", t_dn)

            vn = sV.inc(1)
            v_down_done[(e, hs)] = vn

            def v_dn(v, hs=hs, bank0=bank0, tn=tn):
                v.wait_ge(sT.h, tn)
                v.tensor_tensor(out=ACC[:, hs, :], in0=ACC[:, hs, :],
                                in1=p2(bank0), op=ALU.add).then_inc(sV.h, 1)
            em("v", v_dn)

    PH2_V_END = sV.n
    PH2_T_END = sT.n
    wisem = [qG[0], qG[1], qU[0], qU[1]]

    g_s0 = sS.inc(1)

    def s_g0(s, w=PH2_V_END):
        s.wait_ge(sV.h, w)
        s.activation(out=g0scr[:], in_=ACC[:, 0, :], func=AF.Silu,
                     bias=0.0, scale=1.0).then_inc(sS.h, 1)
    em("s", s_g0)

    g_v0c = sV.inc(1)

    def v_g0(v, w=g_s0):
        v.wait_ge(sS.h, w)
        v.tensor_copy(g_v[:, 0, :], g0scr[:]).then_inc(sV.h, 1)
    em("v", v_g0)

    for hs in range(1, KC):
        gsn = sS.inc(1)

        def s_gc(s, hs=hs, w=g_v0c):
            if hs == 1:
                s.wait_ge(sV.h, w)
            s.activation(out=g_v[:, hs, :], in_=ACC[:, hs, :], func=AF.Silu,
                         bias=0.0, scale=1.0).then_inc(sS.h, 1)
        em("s", s_gc)
    G_DONE_S = sS.n

    pe_wi_done = {}
    v_proc_done = {}
    v_xc_done = {}
    s_gate_done = {}
    s_sx_done = {}
    pe_wo_done = {}
    v_sg_done = {}
    v_y_done = {}

    def xipair(hf_set):
        return (PSA if hf_set == 0 else PSB)[:, 0:515]

    for hf in range(2):
        c0 = hf * 512
        for si in range(64):
            pw = si % 4
            i = si // 2
            is_xi = (si % 2 == 0)
            ds = i if is_xi else 32 + i
            wi_free = pe_wi_done.get((hf, si - 4))
            if wi_free is None and hf > 0:
                wi_free = pe_wi_done.get((hf - 1, 60 + si))

            def g_wi(g, ds=ds, pw=pw, wi_free=wi_free, si=si, hf=hf,
                     wt=PH2_T_END):
                if hf == 0 and si < 4:
                    g.wait_ge(sT.h, wt)
                if wi_free is not None:
                    g.wait_ge(sT.h, wi_free)
                g.dma_start(out=wibuf_v[pw][:], in_=d_wi[ds]).then_inc(wisem[pw].h, 16)
            em("g", g_wi)
            ww = wisem[pw].inc(16)

            bset = i % 2
            bb = 0 if bset == 0 else 4
            if is_xi:
                bank_free = v_xc_done.get((hf, i - 2))
                if bank_free is None and hf > 0:
                    bank_free = v_xc_done.get((hf - 1, 30 + i))
            else:
                bank_free = s_gate_done.get((hf, i - 2))
                if bank_free is None and hf > 0:
                    bank_free = s_gate_done.get((hf - 1, 30 + i))
            y_prev = v_y_done.get(4 * hf - 1) if (si < 6 and hf > 0) else None
            ph2_bar = (hf == 0 and si < 6)

            tn = sT.inc(1)
            pe_wi_done[(hf, si)] = tn

            def t_ip(t, hf=hf, si=si, i=i, is_xi=is_xi, pw=pw, bb=bb, ww=ww,
                     bank_free=bank_free, y_prev=y_prev, ph2_bar=ph2_bar,
                     c0=c0, wv=PH2_V_END):
                t.wait_ge(wisem[pw].h, ww)
                if ph2_bar:
                    t.wait_ge(sV.h, wv)
                if y_prev is not None:
                    t.wait_ge(sV.h, y_prev)
                if bank_free is not None:
                    if is_xi:
                        t.wait_ge(sV.h, bank_free)
                    else:
                        t.wait_ge(sS.h, bank_free)
                last = None
                if is_xi:
                    d0 = pb(bb)
                    d1 = pb(bb + 1)[:, 0:3]
                    for k in range(KC):
                        wk = wibuf_v[pw][:, k, :]
                        t.matmul(d0, wk, xnT[:, k, c0:c0 + 512],
                                 start=(k == 0), stop=(k == KC - 1))
                        last = t.matmul(d1, wk, xnT[:, k, c0 + 512:c0 + 515],
                                        start=(k == 0), stop=(k == KC - 1))
                else:
                    d0 = pb(bb + 2)
                    for k in range(KC):
                        last = t.matmul(d0, wibuf_v[pw][:, k, :],
                                        xnT[:, k, c0 + 3:c0 + 515],
                                        start=(k == 0), stop=(k == KC - 1))
                last.then_inc(sT.h, 1)
            em("t", t_ip)

            if is_xi:
                vxc = sV.inc(1)
                v_xc_done[(hf, i)] = vxc
                cv_free = s_sx_done.get((hf, i - 1))
                if cv_free is None and hf > 0:
                    cv_free = s_sx_done.get((hf - 1, 31 + i))

                def v_cv(v, hf=hf, i=i, bset=bset, tn=tn, cv_free=cv_free):
                    v.wait_ge(sT.h, tn)
                    if cv_free is not None:
                        v.wait_ge(sS.h, cv_free)
                    ps = xipair(bset)
                    cw4 = i * 4
                    v.tensor_scalar(out=cv0_v, in0=ps[:, 0:512],
                                    scalar1=cw_sb[:, cw4:cw4 + 1], scalar2=None,
                                    op0=ALU.mult)
                    v.scalar_tensor_tensor(out=cv1_v, in0=ps[:, 1:513],
                                           scalar=cw_sb[:, cw4 + 1:cw4 + 2],
                                           in1=cv0_v, op0=ALU.mult, op1=ALU.add)
                    v.scalar_tensor_tensor(out=cv0_v, in0=ps[:, 2:514],
                                           scalar=cw_sb[:, cw4 + 2:cw4 + 3],
                                           in1=cv1_v, op0=ALU.mult, op1=ALU.add)
                    v.scalar_tensor_tensor(out=cv1_v, in0=ps[:, 3:515],
                                           scalar=cw_sb[:, cw4 + 3:cw4 + 4],
                                           in1=cv0_v, op0=ALU.mult,
                                           op1=ALU.add).then_inc(sV.h, 1)
                em("v", v_cv)

                ssx = sS.inc(1)
                s_sx_done[(hf, i)] = ssx

                def s_sx(s, i=i, vxc=vxc):
                    s.wait_ge(sV.h, vxc)
                    s.activation(out=sx_v, in_=cv1_v, func=AF.Silu,
                                 bias=cb_sb[:, i:i + 1], scale=1.0).then_inc(sS.h, 1)
                em("s", s_sx)
            else:
                sgt = sS.inc(1)
                s_gate_done[(hf, i)] = sgt
                hprev = v_proc_done.get((hf, i - 1))
                if hprev is None and hf > 0:
                    hprev = v_proc_done.get((hf - 1, 31 + i))

                def s_sg(s, bb=bb, tn=tn, hprev=hprev):
                    s.wait_ge(sT.h, tn)
                    if hprev is not None:
                        s.wait_ge(sV.h, hprev)
                    s.activation(out=sgx_v, in_=pb(bb + 2),
                                 func=AF.Silu, bias=0.0, scale=1.0).then_inc(sS.h, 1)
                em("s", s_sg)

                vh = sV.inc(1)
                v_proc_done[(hf, i)] = vh
                ssx = s_sx_done[(hf, i)]

                def v_h(v, i=i, ssx=ssx, sgt=sgt):
                    v.wait_ge(sS.h, max(ssx, sgt))
                    v.scalar_tensor_tensor(out=h_v[:, i, :], in0=sgx_v,
                                           scalar=dv_sb[:, i:i + 1], in1=sx_v,
                                           op0=ALU.mult,
                                           op1=ALU.mult).then_inc(sV.h, 1)
                em("v", v_h)

        H_ALL = v_proc_done[(hf, 31)]

        for cs in range(KC):
            po = cs % 2
            wo_free = pe_wo_done.get((hf, cs - 2))
            if wo_free is None and hf > 0:
                wo_free = pe_wo_done.get((hf - 1, 14 + cs))

            def g_wo(g, cs=cs, po=po, wo_free=wo_free, hf=hf,
                     wt=PH2_T_END, wv=PH2_V_END):
                if hf == 0 and cs < 2:
                    g.wait_ge(sT.h, wt)
                    g.wait_ge(sV.h, wv)
                if wo_free is not None:
                    g.wait_ge(sT.h, wo_free)
                g.dma_start(out=wobuf_v[po][:], in_=d_wo[cs]).then_inc(qO[po].h, 16)
            em("g", g_wo)
            wo = qO[po].inc(16)

            sbank = 3 if cs % 2 == 0 else 7
            sb_free = v_sg_done.get((hf, cs - 2))
            if sb_free is None and hf > 0:
                sb_free = max(v_sg_done.get((hf - 1, 14 + cs)),
                              v_y_done[4 * hf - 1])
            if sb_free is None:
                sb_free = PH2_V_END

            tn = sT.inc(1)
            pe_wo_done[(hf, cs)] = tn

            def t_op(t, cs=cs, po=po, sbank=sbank, wo=wo, sb_free=sb_free,
                     h_all=H_ALL):
                t.wait_ge(qO[po].h, wo)
                if cs == 0:
                    t.wait_ge(sV.h, h_all)
                t.wait_ge(sV.h, sb_free)
                dst = pb(sbank)
                last = None
                for k in range(32):
                    last = t.matmul(dst, wobuf_v[po][:, k, :], h_v[:, k, :],
                                    start=(k == 0), stop=(k == 31))
                last.then_inc(sT.h, 1)
            em("t", t_op)

            vsg = sV.inc(1)
            v_sg_done[(hf, cs)] = vsg

            def v_sg(v, hf=hf, cs=cs, sbank=sbank, tn=tn, gd=G_DONE_S):
                v.wait_ge(sT.h, tn)
                if hf == 0 and cs == 0:
                    v.wait_ge(sS.h, gd)
                v.tensor_tensor(out=sg_v[:, cs, :], in0=pb(sbank),
                                in1=g_v[:, cs, hf * 512:hf * 512 + 512],
                                op=ALU.mult).then_inc(sV.h, 1)
            em("v", v_sg)

        SG_ALL = v_sg_done[(hf, 15)]

        for tb in range(4):
            tb_g = 4 * hf + tb
            ypt = PSA if tb_g % 2 == 0 else PSB

            def q_xr(q_, tb_g=tb_g):
                if tb_g > 0:
                    q_.wait_ge(qY.h, 16 * tb_g)
                q_.dma_start(out=xres_v[:],
                             in_=d_xh[3 + tb_g * P:3 + tb_g * P + P, :]
                             ).then_inc(qR[0].h, 16)
            em("q", q_xr)
            wr = qR[0].inc(16)

            y_free = v_y_done.get(tb_g - 2)
            tn = sT.inc(1)

            def t_y(t, tb=tb, ypt=ypt, sg_all=SG_ALL, y_free=y_free):
                t.wait_ge(sV.h, sg_all)
                if y_free is not None:
                    t.wait_ge(sV.h, y_free)
                last = None
                for cs in range(KC):
                    last = t.transpose(ypt[:, cs * P:(cs + 1) * P],
                                       sg_v[:, cs, tb * P:(tb + 1) * P], identf[:])
                last.then_inc(sT.h, 1)
            em("t", t_y)

            vy = sV.inc(1)
            v_y_done[tb_g] = vy

            def v_y(v, tb_g=tb_g, ypt=ypt, tn=tn, wr=wr):
                v.wait_ge(sT.h, tn)
                v.wait_ge(qR[0].h, wr)
                if tb_g > 0:
                    v.wait_ge(qY.h, 16 * tb_g)
                v.tensor_tensor(out=y_v[:], in0=ypt[:, 0:2048], in1=xres_v[:],
                                op=ALU.add).then_inc(sV.h, 1)
            em("v", v_y)

            def q_y(q_, tb_g=tb_g, vy=vy):
                q_.wait_ge(sV.h, vy)
                q_.dma_start(out=d_y[tb_g * P:(tb_g + 1) * P, :],
                             in_=y_v[:]).then_inc(qY.h, 16)
            em("q", q_y)
            qY.inc(16)

    Y_ALL = qY.n

    def q_fin(q_, w=Y_ALL):
        q_.wait_ge(qY.h, w)
    em("q", q_fin)

    class _InstrProxy:
        def __init__(self, ae, instr):
            self._ae = ae
            self._instr = instr

        def then_inc(self, sem, k):
            self._instr.then_inc(sem, k)
            self._ae.note_explicit(self, sem, k)
            return self

    class AutoEdge:

        def __init__(self, eng, auto_sem):
            self._e = eng
            self._auto = auto_sem
            self._auto_n = 0
            self._cnt = {}
            self._pend = None
            self._tok = None

        def note_explicit(self, proxy, sem, k):
            if self._pend is not None and self._pend._instr is proxy._instr:
                c = self._cnt.get(id(sem), 0) + k
                self._cnt[id(sem)] = c
                self._tok = (sem, c)
                self._pend = None

        def _barrier(self):
            if self._pend is not None:
                self._auto_n += 1
                self._pend._instr.then_inc(self._auto, 1)
                self._tok = (self._auto, self._auto_n)
                self._pend = None
            if self._tok is not None:
                self._e.wait_ge(*self._tok)
                self._tok = None

        def wait_ge(self, sem, val):
            return self._e.wait_ge(sem, val)

        def __getattr__(self, name):
            real = getattr(self._e, name)

            def f(*a, **kw):
                self._barrier()
                instr = real(*a, **kw)
                p = _InstrProxy(self, instr)
                self._pend = p
                return p
            return f

    with nc.Block() as block:
        @block.tensor
        def _(t):
            for f in OPS["t"]:
                f(t)

        @block.vector
        def _(v):
            ae = AutoEdge(v, sVa.h)
            for f in OPS["v"]:
                f(ae)

        @block.scalar
        def _(s):
            ae = AutoEdge(s, sSa.h)
            for f in OPS["s"]:
                f(ae)

        @block.sync
        def _(q):
            for f in OPS["q"]:
                f(q)

        @block.gpsimd
        def _(g):
            for f in OPS["g"]:
                f(g)

    nc._keepalive = st
    return nc


_NC_CACHE = None


def _get_nc():
    global _NC_CACHE
    if _NC_CACHE is None:
        _NC_CACHE = build_program()
    return _NC_CACHE


def kernel(L0_attn_x, L0_attn_w_in, L0_attn_w_out, L0_attn_conv_w, L0_attn_conv_b,
           L0_attn_D, L0_ffn_router_weight, L0_ffn_router_bias, L0_ffn_gate_up_proj,
           L0_ffn_gate_up_proj_bias, L0_ffn_down_proj, L0_ffn_down_proj_bias):
    f32 = np.float32
    C = lambda a: np.ascontiguousarray(a, dtype=f32)

    x = np.asarray(L0_attn_x, dtype=f32).reshape(8192, D)
    w_in = np.asarray(L0_attn_w_in, dtype=f32)
    w_out = np.asarray(L0_attn_w_out, dtype=f32)
    conv_w = np.asarray(L0_attn_conv_w, dtype=f32)
    conv_b = np.asarray(L0_attn_conv_b, dtype=f32)
    Dvec = np.asarray(L0_attn_D, dtype=f32)
    rw = np.asarray(L0_ffn_router_weight, dtype=f32)
    rbias = np.asarray(L0_ffn_router_bias, dtype=f32)
    gup = np.asarray(L0_ffn_gate_up_proj, dtype=f32)
    gup_b = np.asarray(L0_ffn_gate_up_proj_bias, dtype=f32)
    down = np.asarray(L0_ffn_down_proj, dtype=f32)
    down_b = np.asarray(L0_ffn_down_proj_bias, dtype=f32)

    Wi = C(w_in.reshape(64, P, KC, P).transpose(0, 3, 2, 1))
    Wo = C(w_out.reshape(16, P, 32, P).transpose(0, 3, 2, 1))
    gate = gup[:, :, 0::2]
    up = gup[:, :, 1::2]
    G = C(gate.reshape(E, KC, P, 16, P).transpose(0, 3, 2, 1, 4))
    U = C(up.reshape(E, KC, P, 16, P).transpose(0, 3, 2, 1, 4))
    Dn = C(down.reshape(E, KC, P, 16, P).transpose(0, 3, 2, 1, 4))
    Rt = C(rw.reshape(E, KC, P).transpose(2, 1, 0))
    Rb = C(rbias.reshape(E, 1))
    Db = C(down_b)
    Bg = C(gup_b[:, 0::2].reshape(E, 16, P).transpose(2, 0, 1).reshape(P, 256))
    Bu = C(gup_b[:, 1::2].reshape(E, 16, P).transpose(2, 0, 1).reshape(P, 256))
    Cw = C(conv_w[:, 0, :].reshape(32, P, 4).transpose(1, 0, 2).reshape(P, P))
    Cb = C(conv_b.reshape(32, P).T)
    Dv = C(Dvec.reshape(32, P).T)

    xpad = np.concatenate([np.zeros((3, D), f32), x], axis=0)

    shared = dict(wi=Wi, wo=Wo, gw=G, uw=U, dw=Dn, rt=Rt, rb=Rb, db=Db,
                  bg=Bg, bu=Bu, cw=Cw, cb=Cb, dv=Dv)
    in_maps = []
    for c in range(8):
        m = dict(shared)
        m["xh"] = C(xpad[T * c:T * c + TH])
        in_maps.append(m)

    nc = _get_nc()
    res = run_bass_kernel_spmd(nc, in_maps, core_ids=list(range(8)))
    y = np.concatenate([res.results[c]["y"] for c in range(8)], axis=0)
    return y.reshape(1, 8192, D)


# version 05
# speedup vs baseline: 1.2909x; 1.0268x over previous
import numpy as np
from contextlib import ExitStack

import concourse.bass as bass
import concourse.mybir as mybir
from concourse.bass_utils import run_bass_kernel_spmd

F32 = mybir.dt.float32
BF16 = mybir.dt.bfloat16
AF = mybir.ActivationFunctionType
ALU = mybir.AluOpType
AX = mybir.AxisListType.X

P = 128
T = 1024
TH = 1027
D = 2048
KC = 16
E = 16
ALPHA = 1.702
INVA = float(np.float32(1.0) / np.float32(1.702))
EPS = 1e-5
BIG = 1.0e30
TQ = 256
NQ = 4


class Sem:
    def __init__(self, nc, st, name):
        self.h = st.enter_context(nc.semaphore(name))
        self.n = 0

    def inc(self, k=1):
        self.n += k
        return self.n


def build_program():
    nc = bass.Bass("TRN2", target_bir_lowering=False, debug=False)

    d_xh = nc.dram_tensor("xh", [TH, D], F32, kind="ExternalInput").ap()
    d_wi = nc.dram_tensor("wi", [64, P, KC, P], F32, kind="ExternalInput").ap()
    d_wo = nc.dram_tensor("wo", [16, P, 32, P], F32, kind="ExternalInput").ap()
    d_g = nc.dram_tensor("gw", [E, 16, P, KC, P], F32, kind="ExternalInput").ap()
    d_u = nc.dram_tensor("uw", [E, 16, P, KC, P], F32, kind="ExternalInput").ap()
    d_dn = nc.dram_tensor("dw", [E, 16, P, KC, P], F32, kind="ExternalInput").ap()
    d_rt = nc.dram_tensor("rt", [P, KC, E], F32, kind="ExternalInput").ap()
    d_rb = nc.dram_tensor("rb", [E, 1], F32, kind="ExternalInput").ap()
    d_db = nc.dram_tensor("db", [E, D], F32, kind="ExternalInput").ap()
    d_bg = nc.dram_tensor("bg", [P, 256], F32, kind="ExternalInput").ap()
    d_bu = nc.dram_tensor("bu", [P, 256], F32, kind="ExternalInput").ap()
    d_cw = nc.dram_tensor("cw", [P, P], F32, kind="ExternalInput").ap()
    d_cb = nc.dram_tensor("cb", [P, 32], F32, kind="ExternalInput").ap()
    d_dv = nc.dram_tensor("dv", [P, 32], F32, kind="ExternalInput").ap()
    d_y = nc.dram_tensor("y", [T, D], F32, kind="ExternalOutput").ap()

    st = ExitStack()
    e_ = st.enter_context

    xnT = e_(nc.sbuf_tensor("xnT", [P, KC, 1028], BF16))
    ACC = e_(nc.sbuf_tensor("ACC", [P, KC, T], F32))
    BSCR = e_(nc.sbuf_tensor("BSCR", [P, 16384], BF16))
    STRM = e_(nc.sbuf_tensor("STRM", [P, 20480], BF16))
    TMPP = e_(nc.sbuf_tensor("TMPP", [P, 10240], BF16))

    identb = e_(nc.sbuf_tensor("identb", [P, P], BF16))
    identf = e_(nc.sbuf_tensor("identf", [P, P], F32))
    rt_sb = e_(nc.sbuf_tensor("rt_sb", [P, KC, E], BF16))
    rb_sb = e_(nc.sbuf_tensor("rb_sb", [E, 1], F32))
    db_sb = e_(nc.sbuf_tensor("db_sb", [E, D], BF16))
    bg_sb = e_(nc.sbuf_tensor("bg_sb", [P, 256], F32))
    bu1_sb = e_(nc.sbuf_tensor("bu1_sb", [P, 256], F32))
    cw_sb = e_(nc.sbuf_tensor("cw_sb", [P, P], F32))
    cb_sb = e_(nc.sbuf_tensor("cb_sb", [P, 32], F32))
    dv_sb = e_(nc.sbuf_tensor("dv_sb", [P, 32], F32))
    sel_sb = e_(nc.sbuf_tensor("sel_sb", [E, 16 * P], BF16))
    eps_sb = e_(nc.sbuf_tensor("eps_sb", [P, 1], F32))
    ln_sc = e_(nc.sbuf_tensor("ln_sc", [P, 8], F32))
    lg_tm = e_(nc.sbuf_tensor("lg_tm", [P, 8, E], F32))
    sc_tm = e_(nc.sbuf_tensor("sc_tm", [P, 8, E], F32))
    tkw = e_(nc.sbuf_tensor("tkw", [P, E], F32))
    tkm = e_(nc.sbuf_tensor("tkm", [P, 8], F32))
    tke = e_(nc.sbuf_tensor("tke", [P, E], F32))
    tke2 = e_(nc.sbuf_tensor("tke2", [P, E], F32))
    scTb = e_(nc.sbuf_tensor("scTb", [E, T], BF16))
    sel_v = sel_sb[:].rearrange("p (a b) -> p a b", a=16)

    PSA = e_(nc.psum_tensor("PSA", [P, 2048], F32))
    PSB = e_(nc.psum_tensor("PSB", [P, 2048], F32))

    def pb(i):
        t = PSA if i < 4 else PSB
        j = i % 4
        return t[:, j * 512:(j + 1) * 512]

    def pbb(i):
        t = (PSA if i < 4 else PSB)[:].bitcast(BF16)
        j = i % 4
        return t[:, j * 1024:(j + 1) * 1024]

    def p2(i):
        t = PSA if i < 4 else PSB
        j = i % 4
        return t[:, j * 512:(j + 2) * 512]

    def carve(base, byte_off, dtype, n0, n1=None):
        esz = 4 if dtype == F32 else 2
        nel = n0 * (n1 or 1)
        v = base[:, byte_off // 2:(byte_off + nel * esz) // 2]
        if dtype == F32:
            v = v.bitcast(F32)
        if n1 is not None:
            v = v.rearrange("p (a b) -> p a b", a=n0)
        return v

    act_v = carve(BSCR, 0, BF16, 16, 1024)
    h_v = carve(BSCR, 0, BF16, 32, 512)
    ACCF = ACC[:].rearrange("p a b -> p (a b)")
    g_v = ACCF[:, 0:8192].bitcast(BF16).rearrange("p (a b) -> p a b", a=16)
    sg_v = ACCF[:, 8192:16384].rearrange("p (a b) -> p a b", a=16)

    xt_v = [carve(STRM, 0, F32, 2048), carve(STRM, 8192, F32, 2048)]
    xn_v = [carve(STRM, 16384, BF16, 2048), carve(STRM, 20480, BF16, 2048)]
    gbuf_v = [carve(STRM, 0, BF16, 16, 128), carve(STRM, 4096, BF16, 16, 128)]
    ubuf_v = [carve(STRM, 8192, BF16, 16, 128), carve(STRM, 12288, BF16, 16, 128)]
    dwbuf_v = [carve(STRM, 16384, BF16, 16, 128), carve(STRM, 20480, BF16, 16, 128)]
    bcast_v = [carve(STRM, 24576, F32, 1024), carve(STRM, 28672, F32, 1024)]
    wibuf_v = [carve(STRM, j * 4096, BF16, 16, 128) for j in range(4)]
    wobuf_v = [carve(STRM, 16384, BF16, 32, 128), carve(STRM, 24576, BF16, 32, 128)]
    xres_v = carve(STRM, 32768, F32, 2048)

    xcen_v = carve(TMPP, 0, F32, 2048)
    sq_v = carve(TMPP, 8192, F32, 2048)
    gc_v = [carve(TMPP, 0, F32, 512), carve(TMPP, 8192, F32, 512)]
    glu_v = [carve(TMPP, 2048, F32, 512), carve(TMPP, 10240, F32, 512)]
    u3_v = [carve(TMPP, 4096, F32, 512), carve(TMPP, 12288, F32, 512)]
    a1_v = [carve(TMPP, 6144, F32, 512), carve(TMPP, 14336, F32, 512)]
    cv0_v = carve(TMPP, 0, F32, 512)
    cv1_v = carve(TMPP, 2048, F32, 512)
    sx_v = carve(TMPP, 4096, F32, 512)
    sgx_v = carve(TMPP, 6144, F32, 512)
    y_v = carve(TMPP, 8192, F32, 2048)
    g0scr = carve(TMPP, 8192, BF16, 1024)
    lgT_sb = carve(TMPP, 0, F32, 1024)[0:16, :]

    qP = Sem(nc, st, "qP")
    qPg = Sem(nc, st, "qPg")
    qX = [Sem(nc, st, "qXa"), Sem(nc, st, "qXb")]
    qG = [Sem(nc, st, "qGa"), Sem(nc, st, "qGb")]
    qU = [Sem(nc, st, "qUa"), Sem(nc, st, "qUb")]
    qD = [Sem(nc, st, "qDa"), Sem(nc, st, "qDb")]
    qW = [Sem(nc, st, "qWa"), Sem(nc, st, "qWb")]
    qO = [Sem(nc, st, "qOa"), Sem(nc, st, "qOb")]
    qR = [Sem(nc, st, "qRa"), Sem(nc, st, "qRb")]
    qY = Sem(nc, st, "qY")
    sT = Sem(nc, st, "sT")
    sV = Sem(nc, st, "sV")
    sS = Sem(nc, st, "sS")
    sG = Sem(nc, st, "sG")
    sVa = Sem(nc, st, "sVa")
    sSa = Sem(nc, st, "sSa")

    OPS = {k: [] for k in ("t", "v", "s", "q", "g")}

    def em(k, fn):
        OPS[k].append(fn)

    def g0(g):
        g.memset(identb[:], 0.0).then_inc(sG.h, 1)
        g.memset(identf[:], 0.0).then_inc(sG.h, 1)
        g.memset(sel_sb[:], 0.0).then_inc(sG.h, 1)
        g.memset(eps_sb[:], EPS).then_inc(sG.h, 1)
        g.wait_ge(sG.h, 4)
        g.affine_select(out=identb[:], in_=identb[:], compare_op=ALU.not_equal,
                        fill=1.0, base=0, pattern=[[-1, P]],
                        channel_multiplier=1).then_inc(sG.h, 1)
        g.affine_select(out=identf[:], in_=identf[:], compare_op=ALU.not_equal,
                        fill=1.0, base=0, pattern=[[-1, P]],
                        channel_multiplier=1).then_inc(sG.h, 1)
        g.affine_select(out=sel_v[:, :, :], in_=sel_v[:, :, :],
                        compare_op=ALU.not_equal, fill=1.0, base=0,
                        pattern=[[-1, 16], [0, P]],
                        channel_multiplier=1).then_inc(sG.h, 1)
        g.dma_start(out=rt_sb[:], in_=d_rt).then_inc(qPg.h, 16)
        g.dma_start(out=db_sb[:], in_=d_db).then_inc(qPg.h, 16)
    em("g", g0)
    sG.inc(6)
    GCONST = sG.inc(1)
    qPg.inc(32)

    def q0(q):
        q.dma_start(out=rb_sb[:], in_=d_rb).then_inc(qP.h, 16)
        q.dma_start(out=bg_sb[:], in_=d_bg).then_inc(qP.h, 16)
        q.dma_start(out=bu1_sb[:], in_=d_bu).then_inc(qP.h, 16)
        q.dma_start(out=cw_sb[:], in_=d_cw).then_inc(qP.h, 16)
        q.dma_start(out=cb_sb[:], in_=d_cb).then_inc(qP.h, 16)
        q.dma_start(out=dv_sb[:], in_=d_dv).then_inc(qP.h, 16)
    em("q", q0)
    qP.inc(96)
    P_ALL = qP.n
    PG_ALL = qPg.n

    def v0(v, w=P_ALL):
        v.wait_ge(qP.h, w)
        v.tensor_scalar(out=bu1_sb[:], in0=bu1_sb[:], scalar1=1.0, scalar2=None,
                        op0=ALU.add)
    em("v", v0)

    nrows = [P] * 8 + [3]
    v_xn = {}
    v_ev = {}
    t_tr = {}
    for j in range(9):
        par = j % 2
        r = nrows[j]

        free = v_xn.get(j - 2)

        def qld(q, j=j, par=par, r=r, free=free):
            if free is not None:
                q.wait_ge(sV.h, free)
            q.dma_start(out=xt_v[par][0:r, :], in_=d_xh[j * P:j * P + r, :]
                        ).then_inc(qX[par].h, 16)
        em("q", qld)
        wx = qX[par].inc(16)

        a2 = sV.inc(2)

        def vln(v, par=par, r=r, wx=wx):
            v.wait_ge(qX[par].h, wx)
            xt = xt_v[par][0:r, :]
            v.reduce_sum(ln_sc[0:r, 0:1], xt, axis=AX).then_inc(sV.h, 1)
            v.tensor_scalar(out=ln_sc[0:r, 1:2], in0=ln_sc[0:r, 0:1],
                            scalar1=-1.0 / D, scalar2=None,
                            op0=ALU.mult).then_inc(sV.h, 1)
        em("v", vln)

        b2 = sS.inc(2)

        def sln1(s, par=par, r=r, wx=wx, a2=a2, j=j):
            if j == 0:
                s.wait_ge(sG.h, GCONST)
                s.wait_ge(qP.h, P_ALL)
            s.wait_ge(qX[par].h, wx)
            s.activation(out=xcen_v[0:r, :], in_=xt_v[par][0:r, :],
                         func=AF.Square,
                         accum_out=ln_sc[0:r, 2:3]).then_inc(sS.h, 1)
            s.wait_ge(sV.h, a2)
            s.activation(out=ln_sc[0:r, 3:4], in_=ln_sc[0:r, 1:2],
                         func=AF.Square).then_inc(sS.h, 1)
        em("s", sln1)

        a3 = sV.inc(1)

        def vvar(v, r=r, b2=b2):
            v.wait_ge(sS.h, b2)
            v.scalar_tensor_tensor(out=ln_sc[0:r, 4:5], in0=ln_sc[0:r, 2:3],
                                   scalar=1.0 / D, in1=ln_sc[0:r, 3:4],
                                   op0=ALU.mult,
                                   op1=ALU.subtract).then_inc(sV.h, 1)
        em("v", vvar)

        b3 = sS.inc(2)

        def sln2(s, r=r, a3=a3):
            s.wait_ge(sV.h, a3)
            s.activation(out=ln_sc[0:r, 5:6], in_=ln_sc[0:r, 4:5], func=AF.Ln,
                         bias=eps_sb[0:r, :], scale=1.0)
            s.activation(out=ln_sc[0:r, 6:7], in_=ln_sc[0:r, 5:6], func=AF.Exp,
                         bias=0.0, scale=-0.5).then_inc(sS.h, 1)
        em("s", sln2)
        sS.inc(-1)

        c1 = sV.inc(1)
        v_xn[j] = c1
        ttr_prev = t_tr.get(j - 2)

        def vxn(v, par=par, r=r, b3=b3, ttr_prev=ttr_prev):
            v.wait_ge(sS.h, b3 - 1)
            if ttr_prev is not None:
                v.wait_ge(sT.h, ttr_prev)
            v.tensor_scalar(out=xn_v[par][0:r, :], in0=xt_v[par][0:r, :],
                            scalar1=ln_sc[0:r, 1:2], scalar2=ln_sc[0:r, 6:7],
                            op0=ALU.add, op1=ALU.mult).then_inc(sV.h, 1)
        em("v", vxn)

        d1 = sT.inc(1)
        t_tr[j] = d1
        ev_prev = v_ev.get(j - 1)

        def ttr(t, par=par, r=r, c1=c1, ev_prev=ev_prev):
            t.wait_ge(sV.h, c1)
            t.wait_ge(sG.h, GCONST)
            if ev_prev is not None:
                t.wait_ge(sV.h, ev_prev)
            xn3 = xn_v[par].rearrange("p (a b) -> p a b", a=KC)
            blk = r if r % 2 == 0 else r + 1
            last = None
            for k in range(KC):
                bank = 0 if k < 8 else 1
                dst = pbb(bank)[:, (k % 8) * blk:(k % 8) * blk + r]
                last = t.transpose(dst, xn3[0:r, k, :], identb[0:r, 0:r])
            last.then_inc(sT.h, 1)
        em("t", ttr)

        e1 = sV.inc(1)
        v_ev[j] = e1

        def vev(v, j=j, r=r, d1=d1):
            v.wait_ge(sT.h, d1)
            blk = r if r % 2 == 0 else r + 1
            for bank in range(2):
                src = pbb(bank)[:, 0:8 * blk].rearrange("p (a b) -> p a b", a=8)
                dst = xnT[:, bank * 8:bank * 8 + 8, j * P:j * P + r]
                v.tensor_copy(dst, src[:, :, 0:r])
            v.nop().then_inc(sV.h, 1)
        em("v", vev)

    V_XN8 = v_xn[8]
    V_EV8 = v_ev[8]

    rT = sT.inc(1)

    def t_lg(t, ev8=V_EV8, pall=PG_ALL):
        t.wait_ge(sV.h, ev8)
        t.wait_ge(qPg.h, pall)
        last = None
        for c2 in range(2):
            dst = pb(2 + c2)[0:16, :]
            for k in range(KC):
                last = t.matmul(dst, rt_sb[:, k, :],
                                xnT[:, k, 3 + c2 * 512:3 + c2 * 512 + 512],
                                start=(k == 0), stop=(k == KC - 1))
        last.then_inc(sT.h, 1)
    em("t", t_lg)

    rV = sV.inc(1)

    def v_lg(v, rT=rT):
        v.wait_ge(sT.h, rT)
        v.tensor_scalar(out=lgT_sb[0:16, :], in0=PSA[0:16, 1024:2048],
                        scalar1=rb_sb[0:16, :], scalar2=None,
                        op0=ALU.add).then_inc(sV.h, 1)
    em("v", v_lg)

    rT2 = sT.inc(1)

    def t_lgt(t, rV=rV):
        t.wait_ge(sV.h, rV)
        last = None
        for j8 in range(8):
            last = t.transpose(pb(4)[:, j8 * 16:(j8 + 1) * 16],
                               lgT_sb[0:16, j8 * P:(j8 + 1) * P], identf[0:16, 0:16])
        last.then_inc(sT.h, 1)
    em("t", t_lgt)

    rV2 = sV.inc(1)

    def v_lgt(v, rT2=rT2):
        v.wait_ge(sT.h, rT2)
        v.tensor_copy(lg_tm[:, :, :], pb(4)[:, 0:P].rearrange("p (a b) -> p a b", a=8))
        v.nop().then_inc(sV.h, 1)
    em("v", v_lgt)

    sc_done_v = None
    for j8 in range(8):
        edges = [sV.inc(1) for _ in range(4)]
        mdone = edges[3]

        def v_tk1(v, j8=j8, rV2=rV2, edges=edges):
            if j8 == 0:
                v.wait_ge(sV.h, rV2)
            v.tensor_copy(tkw[:], lg_tm[:, j8, :])
            v.reduce_max(tkm[:, 4:5], tkw[:], axis=AX, negate=True).then_inc(sV.h, 1)
            v.wait_ge(sV.h, edges[0])
            v.tensor_scalar(out=tkm[:, 0:1], in0=tkm[:, 4:5], scalar1=-1.0,
                            scalar2=None, op0=ALU.mult)
            for i in range(1, 4):
                v.tensor_scalar(out=tke[:], in0=tkw[:], scalar1=tkm[:, i - 1:i],
                                scalar2=None, op0=ALU.is_ge)
                v.scalar_tensor_tensor(out=tkw[:], in0=tke[:], scalar=-BIG,
                                       in1=tkw[:], op0=ALU.mult, op1=ALU.add)
                v.reduce_max(tkm[:, i:i + 1], tkw[:], axis=AX).then_inc(sV.h, 1)
                v.wait_ge(sV.h, edges[i])
            v.tensor_scalar(out=tke[:], in0=lg_tm[:, j8, :], scalar1=tkm[:, 3:4],
                            scalar2=None, op0=ALU.is_ge)
            v.nop().then_inc(sV.h, 1)
        mask_done = sV.inc(1)
        em("v", v_tk1)

        exp_done = sS.inc(1)

        def s_tk(s, j8=j8, mask_done=mask_done):
            s.wait_ge(sV.h, mask_done)
            s.activation(out=tke2[:], in_=lg_tm[:, j8, :], func=AF.Exp,
                         bias=tkm[:, 4:5], scale=1.0).then_inc(sS.h, 1)
        em("s", s_tk)

        eden = sV.inc(1)
        erden = sV.inc(1)
        sc_done_v = sV.inc(1)

        def v_tk2(v, j8=j8, exp_done=exp_done, eden=eden, erden=erden):
            v.wait_ge(sS.h, exp_done)
            v.tensor_tensor(out=tke2[:], in0=tke2[:], in1=tke[:], op=ALU.mult)
            v.reduce_sum(tkm[:, 5:6], tke2[:], axis=AX).then_inc(sV.h, 1)
            v.wait_ge(sV.h, eden)
            v.reciprocal(tkm[:, 6:7], tkm[:, 5:6]).then_inc(sV.h, 1)
            v.wait_ge(sV.h, erden)
            v.tensor_scalar(out=sc_tm[:, j8, :], in0=tke2[:], scalar1=tkm[:, 6:7],
                            scalar2=None, op0=ALU.mult).then_inc(sV.h, 1)
        em("v", v_tk2)

    SC_TM_DONE = sc_done_v

    rT3 = sT.inc(1)

    def t_sc(t, w=SC_TM_DONE):
        t.wait_ge(sV.h, w)
        last = None
        for j8 in range(8):
            bank = 5 if j8 < 4 else 6
            last = t.transpose(pb(bank)[0:16, (j8 % 4) * 128:(j8 % 4) * 128 + 128],
                               sc_tm[:, j8, :], identf[:])
        last.then_inc(sT.h, 1)
    em("t", t_sc)

    SCT_DONE = sV.inc(1)

    def v_sc(v, rT3=rT3):
        v.wait_ge(sT.h, rT3)
        v.tensor_copy(scTb[0:16, 0:512], pb(5)[0:16, :])
        v.tensor_copy(scTb[0:16, 512:1024], pb(6)[0:16, :])
        v.nop().then_inc(sV.h, 1)
    em("v", v_sc)

    acc_init_t = {}
    acc_init_v = {}
    for hs in range(KC):
        bank0 = (hs % 2) * 2
        tn = sT.inc(1)
        acc_init_t[hs] = tn
        prev_ev = acc_init_v.get(hs - 2)

        def t_ai(t, hs=hs, bank0=bank0, prev_ev=prev_ev, w=SCT_DONE):
            if hs == 0:
                t.wait_ge(sV.h, w)
            if prev_ev is not None:
                t.wait_ge(sV.h, prev_ev)
            last = None
            for c2 in range(2):
                last = t.matmul(pb(bank0 + c2),
                                db_sb[0:16, hs * P:(hs + 1) * P],
                                scTb[0:16, c2 * 512:(c2 + 1) * 512],
                                start=True, stop=True)
            last.then_inc(sT.h, 1)
        em("t", t_ai)

        vn = sV.inc(1)
        acc_init_v[hs] = vn

        def v_ai(v, hs=hs, bank0=bank0, tn=tn):
            v.wait_ge(sT.h, tn)
            v.tensor_copy(ACC[:, hs, :], p2(bank0))
            v.nop().then_inc(sV.h, 1)
        em("v", v_ai)
    ACC_INIT_V = acc_init_v[15]
    ACC_INIT_V14 = acc_init_v[14]

    pe_slice_done = {}
    v_act_done = {}
    v_down_done = {}
    pe_down_done = {}
    v_bcast_done = {}
    act_all = {}

    for e in range(E):
        tb = sT.inc(1)
        w_prev = v_act_done.get((e - 1, 15))

        def t_bc(t, e=e, w_prev=w_prev, wsc=SCT_DONE):
            if e == 0:
                t.wait_ge(sV.h, wsc)
            if w_prev is not None:
                t.wait_ge(sV.h, w_prev)
            last = None
            for c2 in range(2):
                last = t.matmul(pb(6 + c2), sel_v[:, e, :],
                                scTb[0:16, c2 * 512:(c2 + 1) * 512],
                                start=True, stop=True)
            last.then_inc(sT.h, 1)
        em("t", t_bc)

        vb = sV.inc(1)
        v_bcast_done[e] = vb

        def v_bc(v, e=e, tb=tb):
            v.wait_ge(sT.h, tb)
            v.tensor_scalar(out=bcast_v[e % 2], in0=PSB[:, 1024:2048],
                            scalar1=INVA, scalar2=None, op0=ALU.mult)
            v.nop().then_inc(sV.h, 1)
        em("v", v_bc)

        for m in range(KC):
            pm = m % 2
            gate_free = pe_slice_done.get((e, m - 2)) if m >= 2 else \
                pe_slice_done.get((e - 1, 14 + m))

            def g_gu(g, e=e, m=m, pm=pm, gate_free=gate_free, vxn8=V_XN8):
                if e == 0 and m < 2:
                    g.wait_ge(sV.h, vxn8)
                if gate_free is not None:
                    g.wait_ge(sT.h, gate_free)
                g.dma_start(out=gbuf_v[pm][:], in_=d_g[e, m]).then_inc(qG[pm].h, 16)
                g.dma_start(out=ubuf_v[pm][:], in_=d_u[e, m]).then_inc(qU[pm].h, 16)
            em("g", g_gu)
            wg = qG[pm].inc(16)
            wu = qU[pm].inc(16)

            base = 4 * (m % 2)
            if e == 0 and m == 0:
                set_free = ACC_INIT_V
            elif m == 0:
                set_free = v_down_done[(e - 1, 15)]
            elif m == 1:
                set_free = vb
            else:
                set_free = v_act_done[(e, m - 2)]

            tn = sT.inc(1)
            pe_slice_done[(e, m)] = tn

            def t_gu(t, e=e, m=m, pm=pm, base=base, wg=wg, wu=wu,
                     set_free=set_free):
                t.wait_ge(qG[pm].h, wg)
                t.wait_ge(qU[pm].h, wu)
                t.wait_ge(sV.h, set_free)
                last = None
                for k in range(KC):
                    gw = gbuf_v[pm][:, k, :]
                    uw = ubuf_v[pm][:, k, :]
                    t.matmul(pb(base + 0), gw, xnT[:, k, 3:515],
                             start=(k == 0), stop=(k == KC - 1))
                    t.matmul(pb(base + 1), gw, xnT[:, k, 515:1027],
                             start=(k == 0), stop=(k == KC - 1))
                    t.matmul(pb(base + 2), uw, xnT[:, k, 3:515],
                             start=(k == 0), stop=(k == KC - 1))
                    last = t.matmul(pb(base + 3), uw, xnT[:, k, 515:1027],
                                    start=(k == 0), stop=(k == KC - 1))
                last.then_inc(sT.h, 1)
            em("t", t_gu)

            for tc in range(2):
                sset = tc
                gbank = base + tc
                ubank = base + 2 + tc

                vgc = sV.inc(1)

                def v_gc(v, e=e, m=m, sset=sset, gbank=gbank, tn=tn, tc=tc):
                    if tc == 0:
                        v.wait_ge(sT.h, tn)
                    v.tensor_scalar(out=gc_v[sset], in0=pb(gbank),
                                    scalar1=bg_sb[:, e * 16 + m:e * 16 + m + 1],
                                    scalar2=7.0, op0=ALU.add,
                                    op1=ALU.min).then_inc(sV.h, 1)
                em("v", v_gc)

                sglu = sS.inc(1)

                def s_glu(s, sset=sset, vgc=vgc):
                    s.wait_ge(sV.h, vgc)
                    s.activation(out=glu_v[sset], in_=gc_v[sset], func=AF.Silu,
                                 bias=0.0, scale=ALPHA).then_inc(sS.h, 1)
                em("s", s_glu)

                vact = sV.inc(1)
                if tc == 1:
                    v_act_done[(e, m)] = vact

                def v_act(v, e=e, m=m, tc=tc, sset=sset, ubank=ubank,
                          sglu=sglu, vbc=vb):
                    v.tensor_scalar(out=u3_v[sset], in0=pb(ubank),
                                    scalar1=bu1_sb[:, e * 16 + m:e * 16 + m + 1],
                                    scalar2=8.0, op0=ALU.add, op1=ALU.min)
                    v.wait_ge(sS.h, sglu)
                    v.scalar_tensor_tensor(out=a1_v[sset], in0=u3_v[sset],
                                           scalar=-6.0, in1=glu_v[sset],
                                           op0=ALU.max, op1=ALU.mult)
                    if e == 0 and m == 0 and tc == 0:
                        v.wait_ge(sV.h, vbc)
                    v.tensor_tensor(out=act_v[:, m, tc * 512:(tc + 1) * 512],
                                    in0=a1_v[sset],
                                    in1=bcast_v[e % 2][:, tc * 512:(tc + 1) * 512],
                                    op=ALU.mult).then_inc(sV.h, 1)
                em("v", v_act)
        act_all[e] = v_act_done[(e, 15)]

        for hs in range(KC):
            ph_ = hs % 2
            dw_free = pe_down_done.get((e, hs - 2)) if hs >= 2 else \
                pe_down_done.get((e - 1, 14 + hs))

            def g_dn(g, e=e, hs=hs, ph_=ph_, dw_free=dw_free, vev8=V_EV8):
                if e == 0 and hs < 2:
                    g.wait_ge(sV.h, vev8)
                if dw_free is not None:
                    g.wait_ge(sT.h, dw_free)
                g.dma_start(out=dwbuf_v[ph_][:], in_=d_dn[e, hs]).then_inc(qD[ph_].h, 16)
            em("g", g_dn)
            wd = qD[ph_].inc(16)

            bank0 = 0 if hs % 2 == 0 else 2
            bank_free = v_down_done.get((e, hs - 2))

            tn = sT.inc(1)
            pe_down_done[(e, hs)] = tn

            def t_dn(t, e=e, hs=hs, ph_=ph_, bank0=bank0, wd=wd,
                     bank_free=bank_free, wact14=v_act_done[(e, 14)],
                     wact=act_all[e]):
                t.wait_ge(qD[ph_].h, wd)
                if hs == 0:
                    t.wait_ge(sV.h, wact14)
                if bank_free is not None:
                    t.wait_ge(sV.h, bank_free)
                last = None
                for k in range(KC):
                    if hs == 0 and k == KC - 1:
                        t.wait_ge(sV.h, wact)
                    dw = dwbuf_v[ph_][:, k, :]
                    t.matmul(pb(bank0), dw, act_v[:, k, 0:512],
                             start=(k == 0), stop=(k == KC - 1))
                    last = t.matmul(pb(bank0 + 1), dw, act_v[:, k, 512:1024],
                                    start=(k == 0), stop=(k == KC - 1))
                last.then_inc(sT.h, 1)
            em("t", t_dn)

            vn = sV.inc(1)
            v_down_done[(e, hs)] = vn

            def v_dn(v, hs=hs, bank0=bank0, tn=tn):
                v.wait_ge(sT.h, tn)
                v.tensor_tensor(out=ACC[:, hs, :], in0=ACC[:, hs, :],
                                in1=p2(bank0), op=ALU.add).then_inc(sV.h, 1)
            em("v", v_dn)

    PH2_V_END = sV.n
    PH2_T_END = sT.n
    wisem = [qG[0], qG[1], qU[0], qU[1]]

    g_s0 = sS.inc(1)

    def s_g0(s, w=PH2_V_END):
        s.wait_ge(sV.h, w)
        s.activation(out=g0scr[:], in_=ACC[:, 0, :], func=AF.Silu,
                     bias=0.0, scale=1.0).then_inc(sS.h, 1)
    em("s", s_g0)

    g_v0c = sV.inc(1)

    def v_g0(v, w=g_s0):
        v.wait_ge(sS.h, w)
        v.tensor_copy(g_v[:, 0, :], g0scr[:]).then_inc(sV.h, 1)
    em("v", v_g0)

    for hs in range(1, KC):
        gsn = sS.inc(1)

        def s_gc(s, hs=hs, w=g_v0c):
            if hs == 1:
                s.wait_ge(sV.h, w)
            s.activation(out=g_v[:, hs, :], in_=ACC[:, hs, :], func=AF.Silu,
                         bias=0.0, scale=1.0).then_inc(sS.h, 1)
        em("s", s_gc)
    G_DONE_S = sS.n

    pe_wi_done = {}
    v_proc_done = {}
    v_xc_done = {}
    s_gate_done = {}
    s_sx_done = {}
    pe_wo_done = {}
    v_sg_done = {}
    v_y_done = {}

    def xipair(hf_set):
        return (PSA if hf_set == 0 else PSB)[:, 0:515]

    for hf in range(2):
        c0 = hf * 512
        for si in range(64):
            pw = si % 4
            i = si // 2
            is_xi = (si % 2 == 0)
            ds = i if is_xi else 32 + i
            wi_free = pe_wi_done.get((hf, si - 4))
            if wi_free is None and hf > 0:
                wi_free = pe_wi_done.get((hf - 1, 60 + si))

            def g_wi(g, ds=ds, pw=pw, wi_free=wi_free, si=si, hf=hf,
                     wt=PH2_T_END):
                if hf == 0 and si < 4:
                    g.wait_ge(sT.h, wt)
                if wi_free is not None:
                    g.wait_ge(sT.h, wi_free)
                g.dma_start(out=wibuf_v[pw][:], in_=d_wi[ds]).then_inc(wisem[pw].h, 16)
            em("g", g_wi)
            ww = wisem[pw].inc(16)

            bset = i % 2
            bb = 0 if bset == 0 else 4
            if is_xi:
                bank_free = v_xc_done.get((hf, i - 2))
                if bank_free is None and hf > 0:
                    bank_free = v_xc_done.get((hf - 1, 30 + i))
            else:
                bank_free = s_gate_done.get((hf, i - 2))
                if bank_free is None and hf > 0:
                    bank_free = s_gate_done.get((hf - 1, 30 + i))
            y_prev = v_y_done.get(4 * hf - 1) if (si < 6 and hf > 0) else None
            ph2_bar = (hf == 0 and si < 6)

            tn = sT.inc(1)
            pe_wi_done[(hf, si)] = tn

            def t_ip(t, hf=hf, si=si, i=i, is_xi=is_xi, pw=pw, bb=bb, ww=ww,
                     bank_free=bank_free, y_prev=y_prev, ph2_bar=ph2_bar,
                     c0=c0, wv=PH2_V_END):
                t.wait_ge(wisem[pw].h, ww)
                if ph2_bar:
                    t.wait_ge(sV.h, wv)
                if y_prev is not None:
                    t.wait_ge(sV.h, y_prev)
                if bank_free is not None:
                    if is_xi:
                        t.wait_ge(sV.h, bank_free)
                    else:
                        t.wait_ge(sS.h, bank_free)
                last = None
                if is_xi:
                    d0 = pb(bb)
                    d1 = pb(bb + 1)[:, 0:3]
                    for k in range(KC):
                        wk = wibuf_v[pw][:, k, :]
                        t.matmul(d0, wk, xnT[:, k, c0:c0 + 512],
                                 start=(k == 0), stop=(k == KC - 1))
                        last = t.matmul(d1, wk, xnT[:, k, c0 + 512:c0 + 515],
                                        start=(k == 0), stop=(k == KC - 1))
                else:
                    d0 = pb(bb + 2)
                    for k in range(KC):
                        last = t.matmul(d0, wibuf_v[pw][:, k, :],
                                        xnT[:, k, c0 + 3:c0 + 515],
                                        start=(k == 0), stop=(k == KC - 1))
                last.then_inc(sT.h, 1)
            em("t", t_ip)

            if is_xi:
                vxc = sV.inc(1)
                v_xc_done[(hf, i)] = vxc
                cv_free = s_sx_done.get((hf, i - 1))
                if cv_free is None and hf > 0:
                    cv_free = s_sx_done.get((hf - 1, 31 + i))

                def v_cv(v, hf=hf, i=i, bset=bset, tn=tn, cv_free=cv_free):
                    v.wait_ge(sT.h, tn)
                    if cv_free is not None:
                        v.wait_ge(sS.h, cv_free)
                    ps = xipair(bset)
                    cw4 = i * 4
                    v.tensor_scalar(out=cv0_v, in0=ps[:, 0:512],
                                    scalar1=cw_sb[:, cw4:cw4 + 1], scalar2=None,
                                    op0=ALU.mult)
                    v.scalar_tensor_tensor(out=cv1_v, in0=ps[:, 1:513],
                                           scalar=cw_sb[:, cw4 + 1:cw4 + 2],
                                           in1=cv0_v, op0=ALU.mult, op1=ALU.add)
                    v.scalar_tensor_tensor(out=cv0_v, in0=ps[:, 2:514],
                                           scalar=cw_sb[:, cw4 + 2:cw4 + 3],
                                           in1=cv1_v, op0=ALU.mult, op1=ALU.add)
                    v.scalar_tensor_tensor(out=cv1_v, in0=ps[:, 3:515],
                                           scalar=cw_sb[:, cw4 + 3:cw4 + 4],
                                           in1=cv0_v, op0=ALU.mult,
                                           op1=ALU.add).then_inc(sV.h, 1)
                em("v", v_cv)

                ssx = sS.inc(1)
                s_sx_done[(hf, i)] = ssx

                def s_sx(s, i=i, vxc=vxc):
                    s.wait_ge(sV.h, vxc)
                    s.activation(out=sx_v, in_=cv1_v, func=AF.Silu,
                                 bias=cb_sb[:, i:i + 1], scale=1.0).then_inc(sS.h, 1)
                em("s", s_sx)
            else:
                sgt = sS.inc(1)
                s_gate_done[(hf, i)] = sgt
                hprev = v_proc_done.get((hf, i - 1))
                if hprev is None and hf > 0:
                    hprev = v_proc_done.get((hf - 1, 31 + i))

                def s_sg(s, bb=bb, tn=tn, hprev=hprev):
                    s.wait_ge(sT.h, tn)
                    if hprev is not None:
                        s.wait_ge(sV.h, hprev)
                    s.activation(out=sgx_v, in_=pb(bb + 2),
                                 func=AF.Silu, bias=0.0, scale=1.0).then_inc(sS.h, 1)
                em("s", s_sg)

                vh = sV.inc(1)
                v_proc_done[(hf, i)] = vh
                ssx = s_sx_done[(hf, i)]

                def v_h(v, i=i, ssx=ssx, sgt=sgt):
                    v.wait_ge(sS.h, max(ssx, sgt))
                    v.scalar_tensor_tensor(out=h_v[:, i, :], in0=sgx_v,
                                           scalar=dv_sb[:, i:i + 1], in1=sx_v,
                                           op0=ALU.mult,
                                           op1=ALU.mult).then_inc(sV.h, 1)
                em("v", v_h)

        H_ALL = v_proc_done[(hf, 31)]

        for cs in range(KC):
            po = cs % 2
            wo_free = pe_wo_done.get((hf, cs - 2))
            if wo_free is None and hf > 0:
                wo_free = pe_wo_done.get((hf - 1, 14 + cs))

            def g_wo(g, cs=cs, po=po, wo_free=wo_free, hf=hf,
                     wt=PH2_T_END, wv=PH2_V_END):
                if hf == 0 and cs < 2:
                    g.wait_ge(sT.h, wt)
                    g.wait_ge(sV.h, wv)
                if wo_free is not None:
                    g.wait_ge(sT.h, wo_free)
                g.dma_start(out=wobuf_v[po][:], in_=d_wo[cs]).then_inc(qO[po].h, 16)
            em("g", g_wo)
            wo = qO[po].inc(16)

            sbank = 3 if cs % 2 == 0 else 7
            sb_free = v_sg_done.get((hf, cs - 2))
            if sb_free is None and hf > 0:
                sb_free = max(v_sg_done.get((hf - 1, 14 + cs)),
                              v_y_done[4 * hf - 1])
            if sb_free is None:
                sb_free = PH2_V_END

            tn = sT.inc(1)
            pe_wo_done[(hf, cs)] = tn

            def t_op(t, cs=cs, po=po, sbank=sbank, wo=wo, sb_free=sb_free,
                     h_all=H_ALL):
                t.wait_ge(qO[po].h, wo)
                if cs == 0:
                    t.wait_ge(sV.h, h_all)
                t.wait_ge(sV.h, sb_free)
                dst = pb(sbank)
                last = None
                for k in range(32):
                    last = t.matmul(dst, wobuf_v[po][:, k, :], h_v[:, k, :],
                                    start=(k == 0), stop=(k == 31))
                last.then_inc(sT.h, 1)
            em("t", t_op)

            vsg = sV.inc(1)
            v_sg_done[(hf, cs)] = vsg

            def v_sg(v, hf=hf, cs=cs, sbank=sbank, tn=tn, gd=G_DONE_S):
                v.wait_ge(sT.h, tn)
                if hf == 0 and cs == 0:
                    v.wait_ge(sS.h, gd)
                v.tensor_tensor(out=sg_v[:, cs, :], in0=pb(sbank),
                                in1=g_v[:, cs, hf * 512:hf * 512 + 512],
                                op=ALU.mult).then_inc(sV.h, 1)
            em("v", v_sg)

        SG_ALL = v_sg_done[(hf, 15)]

        for tb in range(4):
            tb_g = 4 * hf + tb
            ypt = PSA if tb_g % 2 == 0 else PSB

            def q_xr(q_, tb_g=tb_g):
                if tb_g > 0:
                    q_.wait_ge(qY.h, 16 * tb_g)
                q_.dma_start(out=xres_v[:],
                             in_=d_xh[3 + tb_g * P:3 + tb_g * P + P, :]
                             ).then_inc(qR[0].h, 16)
            em("q", q_xr)
            wr = qR[0].inc(16)

            y_free = v_y_done.get(tb_g - 2)
            tn = sT.inc(1)

            def t_y(t, tb=tb, ypt=ypt, sg_all=SG_ALL, y_free=y_free):
                t.wait_ge(sV.h, sg_all)
                if y_free is not None:
                    t.wait_ge(sV.h, y_free)
                last = None
                for cs in range(KC):
                    last = t.transpose(ypt[:, cs * P:(cs + 1) * P],
                                       sg_v[:, cs, tb * P:(tb + 1) * P], identf[:])
                last.then_inc(sT.h, 1)
            em("t", t_y)

            vy = sV.inc(1)
            v_y_done[tb_g] = vy

            def v_y(v, tb_g=tb_g, ypt=ypt, tn=tn, wr=wr):
                v.wait_ge(sT.h, tn)
                v.wait_ge(qR[0].h, wr)
                if tb_g > 0:
                    v.wait_ge(qY.h, 16 * tb_g)
                v.tensor_tensor(out=y_v[:], in0=ypt[:, 0:2048], in1=xres_v[:],
                                op=ALU.add).then_inc(sV.h, 1)
            em("v", v_y)

            def q_y(q_, tb_g=tb_g, vy=vy):
                q_.wait_ge(sV.h, vy)
                q_.dma_start(out=d_y[tb_g * P:(tb_g + 1) * P, :],
                             in_=y_v[:]).then_inc(qY.h, 16)
            em("q", q_y)
            qY.inc(16)

    Y_ALL = qY.n

    def q_fin(q_, w=Y_ALL):
        q_.wait_ge(qY.h, w)
    em("q", q_fin)

    class _InstrProxy:
        def __init__(self, ae, instr):
            self._ae = ae
            self._instr = instr

        def then_inc(self, sem, k):
            self._instr.then_inc(sem, k)
            self._ae.note_explicit(self, sem, k)
            return self

    class AutoEdge:

        def __init__(self, eng, auto_sem):
            self._e = eng
            self._auto = auto_sem
            self._auto_n = 0
            self._cnt = {}
            self._pend = None
            self._tok = None

        def note_explicit(self, proxy, sem, k):
            if self._pend is not None and self._pend._instr is proxy._instr:
                c = self._cnt.get(id(sem), 0) + k
                self._cnt[id(sem)] = c
                self._tok = (sem, c)
                self._pend = None

        def _barrier(self):
            if self._pend is not None:
                self._auto_n += 1
                self._pend._instr.then_inc(self._auto, 1)
                self._tok = (self._auto, self._auto_n)
                self._pend = None
            if self._tok is not None:
                self._e.wait_ge(*self._tok)
                self._tok = None

        def wait_ge(self, sem, val):
            return self._e.wait_ge(sem, val)

        def __getattr__(self, name):
            real = getattr(self._e, name)

            def f(*a, **kw):
                self._barrier()
                instr = real(*a, **kw)
                p = _InstrProxy(self, instr)
                self._pend = p
                return p
            return f

    with nc.Block() as block:
        @block.tensor
        def _(t):
            for f in OPS["t"]:
                f(t)

        @block.vector
        def _(v):
            ae = AutoEdge(v, sVa.h)
            for f in OPS["v"]:
                f(ae)

        @block.scalar
        def _(s):
            ae = AutoEdge(s, sSa.h)
            for f in OPS["s"]:
                f(ae)

        @block.sync
        def _(q):
            for f in OPS["q"]:
                f(q)

        @block.gpsimd
        def _(g):
            for f in OPS["g"]:
                f(g)

    nc._keepalive = st
    return nc


_NC_CACHE = None


def _get_nc():
    global _NC_CACHE
    if _NC_CACHE is None:
        _NC_CACHE = build_program()
    return _NC_CACHE


def kernel(L0_attn_x, L0_attn_w_in, L0_attn_w_out, L0_attn_conv_w, L0_attn_conv_b,
           L0_attn_D, L0_ffn_router_weight, L0_ffn_router_bias, L0_ffn_gate_up_proj,
           L0_ffn_gate_up_proj_bias, L0_ffn_down_proj, L0_ffn_down_proj_bias):
    f32 = np.float32
    C = lambda a: np.ascontiguousarray(a, dtype=f32)

    x = np.asarray(L0_attn_x, dtype=f32).reshape(8192, D)
    w_in = np.asarray(L0_attn_w_in, dtype=f32)
    w_out = np.asarray(L0_attn_w_out, dtype=f32)
    conv_w = np.asarray(L0_attn_conv_w, dtype=f32)
    conv_b = np.asarray(L0_attn_conv_b, dtype=f32)
    Dvec = np.asarray(L0_attn_D, dtype=f32)
    rw = np.asarray(L0_ffn_router_weight, dtype=f32)
    rbias = np.asarray(L0_ffn_router_bias, dtype=f32)
    gup = np.asarray(L0_ffn_gate_up_proj, dtype=f32)
    gup_b = np.asarray(L0_ffn_gate_up_proj_bias, dtype=f32)
    down = np.asarray(L0_ffn_down_proj, dtype=f32)
    down_b = np.asarray(L0_ffn_down_proj_bias, dtype=f32)

    Wi = C(w_in.reshape(64, P, KC, P).transpose(0, 3, 2, 1))
    Wo = C(w_out.reshape(16, P, 32, P).transpose(0, 3, 2, 1))
    gate = gup[:, :, 0::2]
    up = gup[:, :, 1::2]
    G = C(gate.reshape(E, KC, P, 16, P).transpose(0, 3, 2, 1, 4))
    U = C(up.reshape(E, KC, P, 16, P).transpose(0, 3, 2, 1, 4))
    Dn = C(down.reshape(E, KC, P, 16, P).transpose(0, 3, 2, 1, 4))
    Rt = C(rw.reshape(E, KC, P).transpose(2, 1, 0))
    Rb = C(rbias.reshape(E, 1))
    Db = C(down_b)
    Bg = C(gup_b[:, 0::2].reshape(E, 16, P).transpose(2, 0, 1).reshape(P, 256))
    Bu = C(gup_b[:, 1::2].reshape(E, 16, P).transpose(2, 0, 1).reshape(P, 256))
    Cw = C(conv_w[:, 0, :].reshape(32, P, 4).transpose(1, 0, 2).reshape(P, P))
    Cb = C(conv_b.reshape(32, P).T)
    Dv = C(Dvec.reshape(32, P).T)

    xpad = np.concatenate([np.zeros((3, D), f32), x], axis=0)

    shared = dict(wi=Wi, wo=Wo, gw=G, uw=U, dw=Dn, rt=Rt, rb=Rb, db=Db,
                  bg=Bg, bu=Bu, cw=Cw, cb=Cb, dv=Dv)
    in_maps = []
    for c in range(8):
        m = dict(shared)
        m["xh"] = C(xpad[T * c:T * c + TH])
        in_maps.append(m)

    nc = _get_nc()
    res = run_bass_kernel_spmd(nc, in_maps, core_ids=list(range(8)))
    y = np.concatenate([res.results[c]["y"] for c in range(8)], axis=0)
    return y.reshape(1, 8192, D)


# version 07 (final)
# speedup vs baseline: 1.2951x; 1.0033x over previous
import numpy as np
from contextlib import ExitStack

import concourse.bass as bass
import concourse.mybir as mybir
from concourse.bass_utils import run_bass_kernel_spmd

F32 = mybir.dt.float32
BF16 = mybir.dt.bfloat16
AF = mybir.ActivationFunctionType
ALU = mybir.AluOpType
AX = mybir.AxisListType.X

P = 128
T = 1024
TH = 1027
D = 2048
KC = 16
E = 16
ALPHA = 1.702
INVA = float(np.float32(1.0) / np.float32(1.702))
EPS = 1e-5
BIG = 1.0e30
TQ = 256
NQ = 4


class Sem:
    def __init__(self, nc, st, name):
        self.h = st.enter_context(nc.semaphore(name))
        self.n = 0

    def inc(self, k=1):
        self.n += k
        return self.n


def build_program():
    nc = bass.Bass("TRN2", target_bir_lowering=False, debug=False)

    d_xh = nc.dram_tensor("xh", [TH, D], F32, kind="ExternalInput").ap()
    d_wi = nc.dram_tensor("wi", [64, P, KC, P], F32, kind="ExternalInput").ap()
    d_wo = nc.dram_tensor("wo", [16, 2, P, 16, P], F32, kind="ExternalInput").ap()
    d_g = nc.dram_tensor("gw", [E, 16, P, KC, P], F32, kind="ExternalInput").ap()
    d_u = nc.dram_tensor("uw", [E, 16, P, KC, P], F32, kind="ExternalInput").ap()
    d_dn = nc.dram_tensor("dw", [E, 16, P, KC, P], F32, kind="ExternalInput").ap()
    d_rt = nc.dram_tensor("rt", [P, KC, E], F32, kind="ExternalInput").ap()
    d_rb = nc.dram_tensor("rb", [E, 1], F32, kind="ExternalInput").ap()
    d_db = nc.dram_tensor("db", [E, D], F32, kind="ExternalInput").ap()
    d_bg = nc.dram_tensor("bg", [P, 256], F32, kind="ExternalInput").ap()
    d_bu = nc.dram_tensor("bu", [P, 256], F32, kind="ExternalInput").ap()
    d_cw = nc.dram_tensor("cw", [P, P], F32, kind="ExternalInput").ap()
    d_cb = nc.dram_tensor("cb", [P, 32], F32, kind="ExternalInput").ap()
    d_dv = nc.dram_tensor("dv", [P, 32], F32, kind="ExternalInput").ap()
    d_y = nc.dram_tensor("y", [T, D], F32, kind="ExternalOutput").ap()

    st = ExitStack()
    e_ = st.enter_context

    xnT = e_(nc.sbuf_tensor("xnT", [P, KC, 1028], BF16))
    ACC = e_(nc.sbuf_tensor("ACC", [P, KC, T], F32))
    BSCR = e_(nc.sbuf_tensor("BSCR", [P, 16384], BF16))
    STRM = e_(nc.sbuf_tensor("STRM", [P, 20480], BF16))
    TMPP = e_(nc.sbuf_tensor("TMPP", [P, 10240], BF16))

    identb = e_(nc.sbuf_tensor("identb", [P, P], BF16))
    identf = e_(nc.sbuf_tensor("identf", [P, P], F32))
    rt_sb = e_(nc.sbuf_tensor("rt_sb", [P, KC, E], BF16))
    rb_sb = e_(nc.sbuf_tensor("rb_sb", [E, 1], F32))
    db_sb = e_(nc.sbuf_tensor("db_sb", [E, D], BF16))
    bg_sb = e_(nc.sbuf_tensor("bg_sb", [P, 256], F32))
    bu1_sb = e_(nc.sbuf_tensor("bu1_sb", [P, 256], F32))
    cw_sb = e_(nc.sbuf_tensor("cw_sb", [P, P], F32))
    cb_sb = e_(nc.sbuf_tensor("cb_sb", [P, 32], F32))
    dv_sb = e_(nc.sbuf_tensor("dv_sb", [P, 32], F32))
    sel_sb = e_(nc.sbuf_tensor("sel_sb", [E, 16 * P], BF16))
    eps_sb = e_(nc.sbuf_tensor("eps_sb", [P, 1], F32))
    ln_sc = e_(nc.sbuf_tensor("ln_sc", [P, 16], F32))
    lg_tm = e_(nc.sbuf_tensor("lg_tm", [P, 8, E], F32))
    sc_tm = e_(nc.sbuf_tensor("sc_tm", [P, 8, E], F32))
    tkw3 = e_(nc.sbuf_tensor("tkw3", [P, 8, E], F32))
    tke3 = e_(nc.sbuf_tensor("tke3", [P, 8, E], F32))
    tkes = e_(nc.sbuf_tensor("tkes", [P, 8, E], F32))
    tkms = e_(nc.sbuf_tensor("tkms", [P, 8, 8], F32))
    scTb = e_(nc.sbuf_tensor("scTb", [E, T], BF16))
    sel_v = sel_sb[:].rearrange("p (a b) -> p a b", a=16)

    PSA = e_(nc.psum_tensor("PSA", [P, 2048], F32))
    PSB = e_(nc.psum_tensor("PSB", [P, 2048], F32))

    def pb(i):
        t = PSA if i < 4 else PSB
        j = i % 4
        return t[:, j * 512:(j + 1) * 512]

    def pbb(i):
        t = (PSA if i < 4 else PSB)[:].bitcast(BF16)
        j = i % 4
        return t[:, j * 1024:(j + 1) * 1024]

    def p2(i):
        t = PSA if i < 4 else PSB
        j = i % 4
        return t[:, j * 512:(j + 2) * 512]

    def carve(base, byte_off, dtype, n0, n1=None):
        esz = 4 if dtype == F32 else 2
        nel = n0 * (n1 or 1)
        v = base[:, byte_off // 2:(byte_off + nel * esz) // 2]
        if dtype == F32:
            v = v.bitcast(F32)
        if n1 is not None:
            v = v.rearrange("p (a b) -> p a b", a=n0)
        return v

    act_v = carve(BSCR, 0, BF16, 16, 1024)
    h_v = carve(BSCR, 0, BF16, 32, 512)
    ACCF = ACC[:].rearrange("p a b -> p (a b)")
    g_v = ACCF[:, 0:8192].bitcast(BF16).rearrange("p (a b) -> p a b", a=16)
    sg_v = ACCF[:, 8192:16384].rearrange("p (a b) -> p a b", a=16)

    xt_v = [carve(STRM, 0, F32, 2048), carve(STRM, 8192, F32, 2048)]
    xn_v = [carve(STRM, 16384, BF16, 2048), carve(STRM, 20480, BF16, 2048)]
    gbuf_v = [carve(STRM, 0, BF16, 16, 128), carve(STRM, 4096, BF16, 16, 128)]
    ubuf_v = [carve(STRM, 8192, BF16, 16, 128), carve(STRM, 12288, BF16, 16, 128)]
    dwbuf_v = [carve(STRM, 16384, BF16, 16, 128), carve(STRM, 20480, BF16, 16, 128)]
    bcast_v = [carve(STRM, 24576, F32, 1024), carve(STRM, 28672, F32, 1024)]
    wibuf_v = [carve(STRM, j * 4096, BF16, 16, 128) for j in range(4)]
    wobuf_v = [carve(STRM, 16384 + j * 4096, BF16, 16, 128) for j in range(4)]
    xres_v = carve(STRM, 32768, F32, 2048)

    xcen_v = carve(TMPP, 0, F32, 2048)
    sq_v = carve(TMPP, 8192, F32, 2048)
    gc_v = [carve(TMPP, 0, F32, 512), carve(TMPP, 8192, F32, 512)]
    glu_v = [carve(TMPP, 2048, F32, 512), carve(TMPP, 10240, F32, 512)]
    u3_v = [carve(TMPP, 4096, F32, 512), carve(TMPP, 12288, F32, 512)]
    a1_v = [carve(TMPP, 6144, F32, 512), carve(TMPP, 14336, F32, 512)]
    cv0_v = carve(TMPP, 0, F32, 512)
    cv1_v = carve(TMPP, 2048, F32, 512)
    sx_v = carve(TMPP, 4096, F32, 512)
    sgx_v = carve(TMPP, 6144, F32, 512)
    y_v = [carve(TMPP, 8192, F32, 1024), carve(TMPP, 12288, F32, 1024)]
    g0scr = carve(TMPP, 8192, BF16, 1024)
    lgT_sb = carve(TMPP, 0, F32, 1024)[0:16, :]

    qP = Sem(nc, st, "qP")
    qPg = Sem(nc, st, "qPg")
    qX = [Sem(nc, st, "qXa"), Sem(nc, st, "qXb")]
    qG = [Sem(nc, st, "qGa"), Sem(nc, st, "qGb")]
    qU = [Sem(nc, st, "qUa"), Sem(nc, st, "qUb")]
    qD = [Sem(nc, st, "qDa"), Sem(nc, st, "qDb")]
    qW = [Sem(nc, st, "qWa"), Sem(nc, st, "qWb")]
    qO = [Sem(nc, st, "qOa"), Sem(nc, st, "qOb")]
    qR = [Sem(nc, st, "qRa"), Sem(nc, st, "qRb")]
    qY = Sem(nc, st, "qY")
    sT = Sem(nc, st, "sT")
    sV = Sem(nc, st, "sV")
    sS = Sem(nc, st, "sS")
    sG = Sem(nc, st, "sG")
    sVa = Sem(nc, st, "sVa")
    sSa = Sem(nc, st, "sSa")

    OPS = {k: [] for k in ("t", "v", "s", "q", "g")}

    def em(k, fn):
        OPS[k].append(fn)

    def g0(g):
        g.memset(identb[:], 0.0).then_inc(sG.h, 1)
        g.memset(identf[:], 0.0).then_inc(sG.h, 1)
        g.memset(sel_sb[:], 0.0).then_inc(sG.h, 1)
        g.memset(eps_sb[:], EPS).then_inc(sG.h, 1)
        g.wait_ge(sG.h, 4)
        g.affine_select(out=identb[:], in_=identb[:], compare_op=ALU.not_equal,
                        fill=1.0, base=0, pattern=[[-1, P]],
                        channel_multiplier=1).then_inc(sG.h, 1)
        g.affine_select(out=identf[:], in_=identf[:], compare_op=ALU.not_equal,
                        fill=1.0, base=0, pattern=[[-1, P]],
                        channel_multiplier=1).then_inc(sG.h, 1)
        g.affine_select(out=sel_v[:, :, :], in_=sel_v[:, :, :],
                        compare_op=ALU.not_equal, fill=1.0, base=0,
                        pattern=[[-1, 16], [0, P]],
                        channel_multiplier=1).then_inc(sG.h, 1)
        g.dma_start(out=rt_sb[:], in_=d_rt).then_inc(qPg.h, 16)
        g.dma_start(out=db_sb[:], in_=d_db).then_inc(qPg.h, 16)
    em("g", g0)
    sG.inc(6)
    GCONST = sG.inc(1)
    qPg.inc(32)

    def q0(q):
        q.dma_start(out=rb_sb[:], in_=d_rb).then_inc(qP.h, 16)
        q.dma_start(out=bg_sb[:], in_=d_bg).then_inc(qP.h, 16)
        q.dma_start(out=bu1_sb[:], in_=d_bu).then_inc(qP.h, 16)
        q.dma_start(out=cw_sb[:], in_=d_cw).then_inc(qP.h, 16)
        q.dma_start(out=cb_sb[:], in_=d_cb).then_inc(qP.h, 16)
        q.dma_start(out=dv_sb[:], in_=d_dv).then_inc(qP.h, 16)
    em("q", q0)
    qP.inc(96)
    P_ALL = qP.n
    PG_ALL = qPg.n

    def v0(v, w=P_ALL):
        v.wait_ge(qP.h, w)
        v.tensor_scalar(out=bu1_sb[:], in0=bu1_sb[:], scalar1=1.0, scalar2=None,
                        op0=ALU.add)
    em("v", v0)

    nrows = [P] * 8 + [3]
    v_xn = {}
    v_ev = {}
    t_tr = {}
    ln_a2 = {}
    ln_b2 = {}
    qx_w = {}

    def emit_stats(j):
        par = j % 2
        r = nrows[j]
        cb = 8 * par
        free = v_xn.get(j - 2)

        def qld(q, j=j, par=par, r=r, free=free):
            if free is not None:
                q.wait_ge(sV.h, free)
            q.dma_start(out=xt_v[par][0:r, :], in_=d_xh[j * P:j * P + r, :]
                        ).then_inc(qX[par].h, 16)
        em("q", qld)
        qx_w[j] = qX[par].inc(16)
        wx = qx_w[j]

        a2 = sV.inc(1)
        ln_a2[j] = a2

        def vln(v, par=par, r=r, wx=wx, cb=cb):
            v.wait_ge(qX[par].h, wx)
            xt = xt_v[par][0:r, :]
            v.reduce_sum(ln_sc[0:r, cb:cb + 1], xt, axis=AX,
                         negate=True).then_inc(sV.h, 1)
        em("v", vln)

        b2 = sS.inc(2)
        ln_b2[j] = b2

        def sln(s, par=par, r=r, wx=wx, a2=a2, cb=cb, j=j):
            if j == 0:
                s.wait_ge(sG.h, GCONST)
                s.wait_ge(qP.h, P_ALL)
            s.wait_ge(qX[par].h, wx)
            s.activation(out=xcen_v[0:r, :], in_=xt_v[par][0:r, :],
                         func=AF.Square,
                         accum_out=ln_sc[0:r, cb + 2:cb + 3]).then_inc(sS.h, 1)
            s.wait_ge(sV.h, a2)
            s.mul(ln_sc[0:r, cb + 1:cb + 2], ln_sc[0:r, cb:cb + 1], 1.0 / D)
            s.activation(out=ln_sc[0:r, cb + 3:cb + 4],
                         in_=ln_sc[0:r, cb:cb + 1], func=AF.Square,
                         bias=0.0, scale=1.0 / D)
            s.mul(ln_sc[0:r, cb + 4:cb + 5], ln_sc[0:r, cb + 3:cb + 4], -float(D))
            s.add(ln_sc[0:r, cb + 5:cb + 6], ln_sc[0:r, cb + 2:cb + 3],
                  ln_sc[0:r, cb + 4:cb + 5])
            s.activation(out=ln_sc[0:r, cb + 6:cb + 7],
                         in_=ln_sc[0:r, cb + 5:cb + 6], func=AF.Ln,
                         bias=eps_sb[0:r, :], scale=1.0 / D)
            s.activation(out=ln_sc[0:r, cb + 7:cb + 8],
                         in_=ln_sc[0:r, cb + 6:cb + 7], func=AF.Exp,
                         bias=0.0, scale=-0.5).then_inc(sS.h, 1)
        em("s", sln)

    def emit_norm(j):
        par = j % 2
        r = nrows[j]
        cb = 8 * par
        c1 = sV.inc(1)
        v_xn[j] = c1
        ttr_prev = t_tr.get(j - 2)

        def vxn(v, par=par, r=r, cb=cb, b2=ln_b2[j], ttr_prev=ttr_prev):
            v.wait_ge(sS.h, b2)
            if ttr_prev is not None:
                v.wait_ge(sT.h, ttr_prev)
            v.tensor_scalar(out=xn_v[par][0:r, :], in0=xt_v[par][0:r, :],
                            scalar1=ln_sc[0:r, cb + 1:cb + 2],
                            scalar2=ln_sc[0:r, cb + 7:cb + 8],
                            op0=ALU.add, op1=ALU.mult).then_inc(sV.h, 1)
        em("v", vxn)

        d1 = sT.inc(1)
        t_tr[j] = d1
        ev_prev = v_ev.get(j - 1)

        def ttr(t, par=par, r=r, c1=c1, ev_prev=ev_prev):
            t.wait_ge(sV.h, c1)
            t.wait_ge(sG.h, GCONST)
            if ev_prev is not None:
                t.wait_ge(sV.h, ev_prev)
            xn3 = xn_v[par].rearrange("p (a b) -> p a b", a=KC)
            blk = r if r % 2 == 0 else r + 1
            last = None
            for k in range(KC):
                bank = 0 if k < 8 else 1
                dst = pbb(bank)[:, (k % 8) * blk:(k % 8) * blk + r]
                last = t.transpose(dst, xn3[0:r, k, :], identb[0:r, 0:r])
            last.then_inc(sT.h, 1)
        em("t", ttr)

        e1 = sV.inc(1)
        v_ev[j] = e1

        def vev(v, j=j, r=r, d1=d1):
            v.wait_ge(sT.h, d1)
            blk = r if r % 2 == 0 else r + 1
            for bank in range(2):
                src = pbb(bank)[:, 0:8 * blk].rearrange("p (a b) -> p a b", a=8)
                dst = xnT[:, bank * 8:bank * 8 + 8, j * P:j * P + r]
                v.tensor_copy(dst, src[:, :, 0:r])
            v.nop().then_inc(sV.h, 1)
        em("v", vev)

    for jj in range(10):
        if jj < 9:
            emit_stats(jj)
        if jj >= 1:
            emit_norm(jj - 1)

    V_XN8 = v_xn[8]
    V_EV8 = v_ev[8]

    rT = sT.inc(1)

    def t_lg(t, ev8=V_EV8, pall=PG_ALL):
        t.wait_ge(sV.h, ev8)
        t.wait_ge(qPg.h, pall)
        last = None
        for c2 in range(2):
            dst = pb(2 + c2)[0:16, :]
            for k in range(KC):
                last = t.matmul(dst, rt_sb[:, k, :],
                                xnT[:, k, 3 + c2 * 512:3 + c2 * 512 + 512],
                                start=(k == 0), stop=(k == KC - 1))
        last.then_inc(sT.h, 1)
    em("t", t_lg)

    rV = sV.inc(1)

    def v_lg(v, rT=rT):
        v.wait_ge(sT.h, rT)
        v.tensor_scalar(out=lgT_sb[0:16, :], in0=PSA[0:16, 1024:2048],
                        scalar1=rb_sb[0:16, :], scalar2=None,
                        op0=ALU.add).then_inc(sV.h, 1)
    em("v", v_lg)

    rT2 = sT.inc(1)

    def t_lgt(t, rV=rV):
        t.wait_ge(sV.h, rV)
        last = None
        for j8 in range(8):
            last = t.transpose(pb(4)[:, j8 * 16:(j8 + 1) * 16],
                               lgT_sb[0:16, j8 * P:(j8 + 1) * P], identf[0:16, 0:16])
        last.then_inc(sT.h, 1)
    em("t", t_lgt)

    rV2 = sV.inc(1)

    def v_lgt(v, rT2=rT2):
        v.wait_ge(sT.h, rT2)
        v.tensor_copy(lg_tm[:, :, :], pb(4)[:, 0:P].rearrange("p (a b) -> p a b", a=8))
        v.nop().then_inc(sV.h, 1)
    em("v", v_lgt)

    def BC(ap):
        return ap.broadcast_to((P, 8, E))

    d_done = sV.inc(3)

    def v_tk1(v, rV2=rV2):
        v.wait_ge(sV.h, rV2)
        v.tensor_copy(tkw3[:], lg_tm[:]).then_inc(sV.h, 1)
        v.tensor_reduce(tkms[:, :, 0:1], tkw3[:], axis=AX,
                        op=ALU.max).then_inc(sV.h, 1)
        v.tensor_tensor(out=tke3[:], in0=lg_tm[:], in1=BC(tkms[:, :, 0:1]),
                        op=ALU.subtract).then_inc(sV.h, 1)
    em("v", v_tk1)

    exp_done = sS.inc(1)

    def s_tk(s, w=d_done):
        s.wait_ge(sV.h, w)
        s.activation(out=tkes[:], in_=tke3[:], func=AF.Exp,
                     bias=0.0, scale=1.0).then_inc(sS.h, 1)
    em("s", s_tk)

    sc_done_v = sV.inc(12)

    def v_tk2(v, exp_done=exp_done):
        v.wait_ge(sS.h, exp_done)
        for i in range(1, 4):
            v.tensor_tensor(out=tke3[:], in0=tkw3[:], in1=BC(tkms[:, :, i - 1:i]),
                            op=ALU.is_ge).then_inc(sV.h, 1)
            v.scalar_tensor_tensor(out=tkw3[:], in0=tke3[:], scalar=-BIG,
                                   in1=tkw3[:], op0=ALU.mult,
                                   op1=ALU.add).then_inc(sV.h, 1)
            v.tensor_reduce(tkms[:, :, i:i + 1], tkw3[:], axis=AX,
                            op=ALU.max).then_inc(sV.h, 1)
        v.tensor_tensor(out=tke3[:], in0=lg_tm[:], in1=BC(tkms[:, :, 3:4]),
                        op=ALU.is_ge).then_inc(sV.h, 1)
        v.tensor_tensor(out=tkes[:], in0=tkes[:], in1=tke3[:],
                        op=ALU.mult).then_inc(sV.h, 1)
        v.tensor_reduce(tkms[:, :, 4:5], tkes[:], axis=AX, op=ALU.add)
        v.reciprocal(tkms[:, :, 5:6], tkms[:, :, 4:5])
        v.tensor_tensor(out=sc_tm[:], in0=tkes[:], in1=BC(tkms[:, :, 5:6]),
                        op=ALU.mult).then_inc(sV.h, 1)
    em("v", v_tk2)

    SC_TM_DONE = sc_done_v

    rT3 = sT.inc(1)

    def t_sc(t, w=SC_TM_DONE):
        t.wait_ge(sV.h, w)
        last = None
        for j8 in range(8):
            bank = 5 if j8 < 4 else 6
            last = t.transpose(pb(bank)[0:16, (j8 % 4) * 128:(j8 % 4) * 128 + 128],
                               sc_tm[:, j8, :], identf[:])
        last.then_inc(sT.h, 1)
    em("t", t_sc)

    SCT_DONE = sV.inc(1)

    def v_sc(v, rT3=rT3):
        v.wait_ge(sT.h, rT3)
        v.tensor_copy(scTb[0:16, 0:512], pb(5)[0:16, :])
        v.tensor_copy(scTb[0:16, 512:1024], pb(6)[0:16, :])
        v.nop().then_inc(sV.h, 1)
    em("v", v_sc)

    acc_init_t = {}
    acc_init_v = {}
    for hs in range(KC):
        bank0 = (hs % 2) * 2
        tn = sT.inc(1)
        acc_init_t[hs] = tn
        prev_ev = acc_init_v.get(hs - 2)

        def t_ai(t, hs=hs, bank0=bank0, prev_ev=prev_ev, w=SCT_DONE):
            if hs == 0:
                t.wait_ge(sV.h, w)
            if prev_ev is not None:
                t.wait_ge(sV.h, prev_ev)
            last = None
            for c2 in range(2):
                last = t.matmul(pb(bank0 + c2),
                                db_sb[0:16, hs * P:(hs + 1) * P],
                                scTb[0:16, c2 * 512:(c2 + 1) * 512],
                                start=True, stop=True)
            last.then_inc(sT.h, 1)
        em("t", t_ai)

        vn = sV.inc(1)
        acc_init_v[hs] = vn

        def v_ai(v, hs=hs, bank0=bank0, tn=tn):
            v.wait_ge(sT.h, tn)
            v.tensor_copy(ACC[:, hs, :], p2(bank0))
            v.nop().then_inc(sV.h, 1)
        em("v", v_ai)
    ACC_INIT_V = acc_init_v[15]
    ACC_INIT_V14 = acc_init_v[14]

    pe_slice_done = {}
    v_act_done = {}
    v_down_done = {}
    pe_down_done = {}
    v_bcast_done = {}
    act_all = {}

    def emit_bcast(e):
        tb = sT.inc(1)
        w_prev = v_act_done.get((e - 1, 15))

        def t_bc(t, e=e, w_prev=w_prev, wsc=SCT_DONE):
            if e == 0:
                t.wait_ge(sV.h, wsc)
            if w_prev is not None:
                t.wait_ge(sV.h, w_prev)
            last = None
            for c2 in range(2):
                last = t.matmul(pb(6 + c2), sel_v[:, e, :],
                                scTb[0:16, c2 * 512:(c2 + 1) * 512],
                                start=True, stop=True)
            last.then_inc(sT.h, 1)
        em("t", t_bc)

        vb = sV.inc(1)
        v_bcast_done[e] = vb

        def v_bc(v, e=e, tb=tb):
            v.wait_ge(sT.h, tb)
            v.tensor_scalar(out=bcast_v[e % 2], in0=PSB[:, 1024:2048],
                            scalar1=INVA, scalar2=None, op0=ALU.mult)
            v.nop().then_inc(sV.h, 1)
        em("v", v_bc)

    emit_bcast(0)
    for e in range(E):
        vb = v_bcast_done[e]

        for m in range(KC):
            pm = m % 2
            gate_free = pe_slice_done.get((e, m - 2)) if m >= 2 else \
                pe_slice_done.get((e - 1, 14 + m))

            def g_gu(g, e=e, m=m, pm=pm, gate_free=gate_free, vxn8=V_XN8):
                if e == 0 and m < 2:
                    g.wait_ge(sV.h, vxn8)
                if gate_free is not None:
                    g.wait_ge(sT.h, gate_free)
                g.dma_start(out=gbuf_v[pm][:], in_=d_g[e, m]).then_inc(qG[pm].h, 16)
                g.dma_start(out=ubuf_v[pm][:], in_=d_u[e, m]).then_inc(qU[pm].h, 16)
            em("g", g_gu)
            wg = qG[pm].inc(16)
            wu = qU[pm].inc(16)

            base = 4 * (m % 2)
            if e == 0 and m == 0:
                set_free = ACC_INIT_V
            elif m == 0:
                set_free = v_down_done[(e - 1, 14)]
            elif m == 1:
                set_free = vb
            else:
                set_free = v_act_done[(e, m - 2)]

            tn = sT.inc(1)
            pe_slice_done[(e, m)] = tn

            def t_gu(t, e=e, m=m, pm=pm, base=base, wg=wg, wu=wu,
                     set_free=set_free):
                t.wait_ge(qG[pm].h, wg)
                t.wait_ge(qU[pm].h, wu)
                t.wait_ge(sV.h, set_free)
                last = None
                for k in range(KC):
                    gw = gbuf_v[pm][:, k, :]
                    uw = ubuf_v[pm][:, k, :]
                    t.matmul(pb(base + 0), gw, xnT[:, k, 3:515],
                             start=(k == 0), stop=(k == KC - 1))
                    t.matmul(pb(base + 1), gw, xnT[:, k, 515:1027],
                             start=(k == 0), stop=(k == KC - 1))
                    t.matmul(pb(base + 2), uw, xnT[:, k, 3:515],
                             start=(k == 0), stop=(k == KC - 1))
                    last = t.matmul(pb(base + 3), uw, xnT[:, k, 515:1027],
                                    start=(k == 0), stop=(k == KC - 1))
                last.then_inc(sT.h, 1)
            em("t", t_gu)

            for tc in range(2):
                sset = tc
                gbank = base + tc
                ubank = base + 2 + tc

                vgc = sV.inc(1)

                def v_gc(v, e=e, m=m, sset=sset, gbank=gbank, tn=tn, tc=tc):
                    if tc == 0:
                        v.wait_ge(sT.h, tn)
                    v.tensor_scalar(out=gc_v[sset], in0=pb(gbank),
                                    scalar1=bg_sb[:, e * 16 + m:e * 16 + m + 1],
                                    scalar2=7.0, op0=ALU.add,
                                    op1=ALU.min).then_inc(sV.h, 1)
                em("v", v_gc)

                sglu = sS.inc(1)

                def s_glu(s, sset=sset, vgc=vgc):
                    s.wait_ge(sV.h, vgc)
                    s.activation(out=glu_v[sset], in_=gc_v[sset], func=AF.Silu,
                                 bias=0.0, scale=ALPHA).then_inc(sS.h, 1)
                em("s", s_glu)

                vact = sV.inc(1)
                if tc == 1:
                    v_act_done[(e, m)] = vact

                def v_act(v, e=e, m=m, tc=tc, sset=sset, ubank=ubank,
                          sglu=sglu, vbc=vb):
                    v.tensor_scalar(out=u3_v[sset], in0=pb(ubank),
                                    scalar1=bu1_sb[:, e * 16 + m:e * 16 + m + 1],
                                    scalar2=8.0, op0=ALU.add, op1=ALU.min)
                    v.wait_ge(sS.h, sglu)
                    v.scalar_tensor_tensor(out=a1_v[sset], in0=u3_v[sset],
                                           scalar=-6.0, in1=glu_v[sset],
                                           op0=ALU.max, op1=ALU.mult)
                    if e == 0 and m == 0 and tc == 0:
                        v.wait_ge(sV.h, vbc)
                    v.tensor_tensor(out=act_v[:, m, tc * 512:(tc + 1) * 512],
                                    in0=a1_v[sset],
                                    in1=bcast_v[e % 2][:, tc * 512:(tc + 1) * 512],
                                    op=ALU.mult).then_inc(sV.h, 1)
                em("v", v_act)
        act_all[e] = v_act_done[(e, 15)]

        down_order = list(range(14)) + [15, 14]
        prev_user = {0: (e - 1, 14), 1: (e - 1, 15)}
        for oi, hs in enumerate(down_order):
            if oi == 14 and e < E - 1:
                emit_bcast(e + 1)
            ph_ = hs % 2
            dw_free = pe_down_done.get(prev_user[ph_])
            prev_user[ph_] = (e, hs)

            def g_dn(g, e=e, hs=hs, ph_=ph_, dw_free=dw_free, vev8=V_EV8):
                if e == 0 and hs < 2:
                    g.wait_ge(sV.h, vev8)
                if dw_free is not None:
                    g.wait_ge(sT.h, dw_free)
                g.dma_start(out=dwbuf_v[ph_][:], in_=d_dn[e, hs]).then_inc(qD[ph_].h, 16)
            em("g", g_dn)
            wd = qD[ph_].inc(16)

            bank0 = 0 if hs % 2 == 0 else 2
            bank_free = v_down_done.get((e, hs - 2))

            tn = sT.inc(1)
            pe_down_done[(e, hs)] = tn

            def t_dn(t, e=e, hs=hs, ph_=ph_, bank0=bank0, wd=wd,
                     bank_free=bank_free, wact14=v_act_done[(e, 14)],
                     wact=act_all[e]):
                t.wait_ge(qD[ph_].h, wd)
                if hs == 0:
                    t.wait_ge(sV.h, wact14)
                if bank_free is not None:
                    t.wait_ge(sV.h, bank_free)
                last = None
                for k in range(KC):
                    if hs == 0 and k == KC - 1:
                        t.wait_ge(sV.h, wact)
                    dw = dwbuf_v[ph_][:, k, :]
                    t.matmul(pb(bank0), dw, act_v[:, k, 0:512],
                             start=(k == 0), stop=(k == KC - 1))
                    last = t.matmul(pb(bank0 + 1), dw, act_v[:, k, 512:1024],
                                    start=(k == 0), stop=(k == KC - 1))
                last.then_inc(sT.h, 1)
            em("t", t_dn)

            vn = sV.inc(1)
            v_down_done[(e, hs)] = vn

            def v_dn(v, hs=hs, bank0=bank0, tn=tn):
                v.wait_ge(sT.h, tn)
                v.tensor_tensor(out=ACC[:, hs, :], in0=ACC[:, hs, :],
                                in1=p2(bank0), op=ALU.add).then_inc(sV.h, 1)
            em("v", v_dn)

    PH2_V_END = sV.n
    PH2_T_END = sT.n
    wisem = [qG[0], qG[1], qU[0], qU[1]]
    wosem = [qO[0], qO[1], qD[0], qD[1]]

    g_s0 = sS.inc(1)

    def s_g0(s, w=PH2_V_END):
        s.wait_ge(sV.h, w)
        s.activation(out=g0scr[:], in_=ACC[:, 0, :], func=AF.Silu,
                     bias=0.0, scale=1.0).then_inc(sS.h, 1)
    em("s", s_g0)

    g_v0c = sV.inc(1)

    def v_g0(v, w=g_s0):
        v.wait_ge(sS.h, w)
        v.tensor_copy(g_v[:, 0, :], g0scr[:]).then_inc(sV.h, 1)
    em("v", v_g0)

    for hs in range(1, KC):
        gsn = sS.inc(1)

        def s_gc(s, hs=hs, w=g_v0c):
            if hs == 1:
                s.wait_ge(sV.h, w)
            s.activation(out=g_v[:, hs, :], in_=ACC[:, hs, :], func=AF.Silu,
                         bias=0.0, scale=1.0).then_inc(sS.h, 1)
        em("s", s_gc)
    G_DONE_S = sS.n

    pe_wi_done = {}
    v_proc_done = {}
    v_xc_done = {}
    s_gate_done = {}
    s_sx_done = {}
    pe_wo_done = {}
    v_sg_done = {}
    v_y_done = {}

    def xipair(hf_set):
        return (PSA if hf_set == 0 else PSB)[:, 0:515]

    for hf in range(2):
        c0 = hf * 512
        for si in range(64):
            pw = si % 4
            i = si // 2
            is_xi = (si % 2 == 0)
            ds = i if is_xi else 32 + i
            wi_free = pe_wi_done.get((hf, si - 4))
            if wi_free is None and hf > 0:
                wi_free = pe_wi_done.get((hf - 1, 60 + si))
            bset = i % 2
            bb = 0 if bset == 0 else 4

            def g_wi(g, ds=ds, pw=pw, wi_free=wi_free, si=si, hf=hf,
                     wt=PH2_T_END):
                if hf == 0 and si < 4:
                    g.wait_ge(sT.h, wt)
                if wi_free is not None:
                    g.wait_ge(sT.h, wi_free)
                g.dma_start(out=wibuf_v[pw][:], in_=d_wi[ds]).then_inc(wisem[pw].h, 16)
            em("g", g_wi)
            ww = wisem[pw].inc(16)

            if is_xi:
                bank_free = v_xc_done.get((hf, i - 2))
                if bank_free is None and hf > 0:
                    bank_free = v_xc_done.get((hf - 1, 30 + i))
            else:
                bank_free = s_gate_done.get((hf, i - 2))
                if bank_free is None and hf > 0:
                    bank_free = s_gate_done.get((hf - 1, 30 + i))
            if si < 6 and hf > 0:
                y_prev = v_y_done.get(4 * hf - 2) if bset == 0 else \
                    v_y_done.get(4 * hf - 1)
            else:
                y_prev = None
            ph2_bar = (hf == 0 and si < 6)

            tn = sT.inc(1)
            pe_wi_done[(hf, si)] = tn

            def t_ip(t, hf=hf, si=si, i=i, is_xi=is_xi, pw=pw, bb=bb, ww=ww,
                     bank_free=bank_free, y_prev=y_prev, ph2_bar=ph2_bar,
                     c0=c0, wv=PH2_V_END):
                t.wait_ge(wisem[pw].h, ww)
                if ph2_bar:
                    t.wait_ge(sV.h, wv)
                if y_prev is not None:
                    t.wait_ge(sV.h, y_prev)
                if bank_free is not None:
                    if is_xi:
                        t.wait_ge(sV.h, bank_free)
                    else:
                        t.wait_ge(sS.h, bank_free)
                last = None
                if is_xi:
                    d0 = pb(bb)
                    d1 = pb(bb + 1)[:, 0:3]
                    for k in range(KC):
                        wk = wibuf_v[pw][:, k, :]
                        t.matmul(d0, wk, xnT[:, k, c0:c0 + 512],
                                 start=(k == 0), stop=(k == KC - 1))
                        last = t.matmul(d1, wk, xnT[:, k, c0 + 512:c0 + 515],
                                        start=(k == 0), stop=(k == KC - 1))
                else:
                    d0 = pb(bb + 2)
                    for k in range(KC):
                        last = t.matmul(d0, wibuf_v[pw][:, k, :],
                                        xnT[:, k, c0 + 3:c0 + 515],
                                        start=(k == 0), stop=(k == KC - 1))
                last.then_inc(sT.h, 1)
            em("t", t_ip)

            if is_xi:
                vxc = sV.inc(1)
                v_xc_done[(hf, i)] = vxc
                cv_free = s_sx_done.get((hf, i - 1))
                if cv_free is None and hf > 0:
                    cv_free = s_sx_done.get((hf - 1, 31 + i))

                def v_cv(v, hf=hf, i=i, bset=bset, tn=tn, cv_free=cv_free):
                    v.wait_ge(sT.h, tn)
                    if cv_free is not None:
                        v.wait_ge(sS.h, cv_free)
                    ps = xipair(bset)
                    cw4 = i * 4
                    v.tensor_scalar(out=cv0_v, in0=ps[:, 0:512],
                                    scalar1=cw_sb[:, cw4:cw4 + 1], scalar2=None,
                                    op0=ALU.mult)
                    v.scalar_tensor_tensor(out=cv1_v, in0=ps[:, 1:513],
                                           scalar=cw_sb[:, cw4 + 1:cw4 + 2],
                                           in1=cv0_v, op0=ALU.mult, op1=ALU.add)
                    v.scalar_tensor_tensor(out=cv0_v, in0=ps[:, 2:514],
                                           scalar=cw_sb[:, cw4 + 2:cw4 + 3],
                                           in1=cv1_v, op0=ALU.mult, op1=ALU.add)
                    v.scalar_tensor_tensor(out=cv1_v, in0=ps[:, 3:515],
                                           scalar=cw_sb[:, cw4 + 3:cw4 + 4],
                                           in1=cv0_v, op0=ALU.mult,
                                           op1=ALU.add).then_inc(sV.h, 1)
                em("v", v_cv)

                ssx = sS.inc(1)
                s_sx_done[(hf, i)] = ssx

                def s_sx(s, i=i, vxc=vxc):
                    s.wait_ge(sV.h, vxc)
                    s.activation(out=sx_v, in_=cv1_v, func=AF.Silu,
                                 bias=cb_sb[:, i:i + 1], scale=1.0).then_inc(sS.h, 1)
                em("s", s_sx)
            else:
                sgt = sS.inc(1)
                s_gate_done[(hf, i)] = sgt
                hprev = v_proc_done.get((hf, i - 1))
                if hprev is None and hf > 0:
                    hprev = v_proc_done.get((hf - 1, 31 + i))

                def s_sg(s, bb=bb, tn=tn, hprev=hprev):
                    s.wait_ge(sT.h, tn)
                    if hprev is not None:
                        s.wait_ge(sV.h, hprev)
                    s.activation(out=sgx_v, in_=pb(bb + 2),
                                 func=AF.Silu, bias=0.0, scale=1.0).then_inc(sS.h, 1)
                em("s", s_sg)

                vh = sV.inc(1)
                v_proc_done[(hf, i)] = vh
                ssx = s_sx_done[(hf, i)]

                def v_h(v, i=i, ssx=ssx, sgt=sgt):
                    v.wait_ge(sS.h, max(ssx, sgt))
                    v.scalar_tensor_tensor(out=h_v[:, i, :], in0=sgx_v,
                                           scalar=dv_sb[:, i:i + 1], in1=sx_v,
                                           op0=ALU.mult,
                                           op1=ALU.mult).then_inc(sV.h, 1)
                em("v", v_h)

        H_ALL = v_proc_done[(hf, 31)]

        for cs in range(KC):
            wo_free = pe_wo_done.get((hf, cs - 2))
            if wo_free is None and hf > 0:
                wo_free = pe_wo_done.get((hf - 1, 14 + cs))
            wo_w = []
            for h2 in range(2):
                bw = (2 * cs + h2) % 4

                def g_wo(g, cs=cs, h2=h2, bw=bw, wo_free=wo_free, hf=hf,
                         wt=PH2_T_END, wv=PH2_V_END):
                    if hf == 0 and cs < 2 and h2 == 0:
                        g.wait_ge(sT.h, wt)
                        g.wait_ge(sV.h, wv)
                    if wo_free is not None and h2 == 0:
                        g.wait_ge(sT.h, wo_free)
                    g.dma_start(out=wobuf_v[bw][:],
                                in_=d_wo[cs, h2]).then_inc(wosem[bw].h, 16)
                em("g", g_wo)
                wo_w.append((bw, wosem[bw].inc(16)))

            sbank = 3 if cs % 2 == 0 else 7
            sb_free = v_sg_done.get((hf, cs - 2))
            if sb_free is None and hf > 0:
                sb_free = max(v_sg_done.get((hf - 1, 14 + cs)),
                              v_y_done[4 * hf - 1])
            if sb_free is None:
                sb_free = PH2_V_END

            tn = sT.inc(1)
            pe_wo_done[(hf, cs)] = tn

            def t_op(t, cs=cs, sbank=sbank, wo_w=wo_w, sb_free=sb_free,
                     h_all=H_ALL):
                if cs == 0:
                    t.wait_ge(sV.h, h_all)
                t.wait_ge(sV.h, sb_free)
                dst = pb(sbank)
                last = None
                for k in range(32):
                    h2 = k // 16
                    if k % 16 == 0:
                        bw, ww2 = wo_w[h2]
                        t.wait_ge(wosem[bw].h, ww2)
                    last = t.matmul(dst, wobuf_v[wo_w[h2][0]][:, k % 16, :],
                                    h_v[:, k, :],
                                    start=(k == 0), stop=(k == 31))
                last.then_inc(sT.h, 1)
            em("t", t_op)

            vsg = sV.inc(1)
            v_sg_done[(hf, cs)] = vsg

            def v_sg(v, hf=hf, cs=cs, sbank=sbank, tn=tn, gd=G_DONE_S):
                v.wait_ge(sT.h, tn)
                if hf == 0 and cs == 0:
                    v.wait_ge(sS.h, gd)
                v.tensor_tensor(out=sg_v[:, cs, :], in0=pb(sbank),
                                in1=g_v[:, cs, hf * 512:hf * 512 + 512],
                                op=ALU.mult).then_inc(sV.h, 1)
            em("v", v_sg)

        SG_ALL = v_sg_done[(hf, 15)]

        for tb in range(4):
            tb_g = 4 * hf + tb
            ypt = PSA if tb_g % 2 == 0 else PSB

            def q_xr(q_, tb_g=tb_g):
                if tb_g > 0:
                    q_.wait_ge(qY.h, 16 * tb_g)
                q_.dma_start(out=xres_v[:],
                             in_=d_xh[3 + tb_g * P:3 + tb_g * P + P, :]
                             ).then_inc(qR[0].h, 16)
            em("q", q_xr)
            wr = qR[0].inc(16)

            y_free = v_y_done.get(tb_g - 2)
            tn = sT.inc(1)

            def t_y(t, tb=tb, ypt=ypt, sg_all=SG_ALL, y_free=y_free):
                t.wait_ge(sV.h, sg_all)
                if y_free is not None:
                    t.wait_ge(sV.h, y_free)
                last = None
                for cs in range(KC):
                    last = t.transpose(ypt[:, cs * P:(cs + 1) * P],
                                       sg_v[:, cs, tb * P:(tb + 1) * P], identf[:])
                last.then_inc(sT.h, 1)
            em("t", t_y)

            vys = {}
            for hc in (1, 0):
                vy = sV.inc(1)
                vys[hc] = vy

                def v_yc(v, tb_g=tb_g, ypt=ypt, tn=tn, wr=wr, hc=hc):
                    if hc == 1:
                        v.wait_ge(sT.h, tn)
                        v.wait_ge(qR[0].h, wr)
                    if tb_g > 0:
                        v.wait_ge((qR[1] if hc == 1 else qY).h, 16 * tb_g)
                    v.tensor_tensor(out=y_v[1 - hc][:],
                                    in0=ypt[:, hc * 1024:(hc + 1) * 1024],
                                    in1=xres_v[:, hc * 1024:(hc + 1) * 1024],
                                    op=ALU.add).then_inc(sV.h, 1)
                em("v", v_yc)

                ysem = qR[1] if hc == 1 else qY

                def q_yc(q_, tb_g=tb_g, vy=vy, hc=hc, ysem=ysem):
                    q_.wait_ge(sV.h, vy)
                    q_.dma_start(out=d_y[tb_g * P:(tb_g + 1) * P,
                                         hc * 1024:(hc + 1) * 1024],
                                 in_=y_v[1 - hc][:]).then_inc(ysem.h, 16)
                em("q", q_yc)
                ysem.inc(16)
            v_y_done[tb_g] = vys[0]

    Y_ALL = qY.n
    YB_ALL = qR[1].n

    def q_fin(q_, w=Y_ALL, wb=YB_ALL):
        q_.wait_ge(qY.h, w)
        q_.wait_ge(qR[1].h, wb)
    em("q", q_fin)

    class _InstrProxy:
        def __init__(self, ae, instr):
            self._ae = ae
            self._instr = instr

        def then_inc(self, sem, k):
            self._instr.then_inc(sem, k)
            self._ae.note_explicit(self, sem, k)
            return self

    class AutoEdge:

        def __init__(self, eng, auto_sem):
            self._e = eng
            self._auto = auto_sem
            self._auto_n = 0
            self._cnt = {}
            self._pend = None
            self._tok = None

        def note_explicit(self, proxy, sem, k):
            if self._pend is not None and self._pend._instr is proxy._instr:
                c = self._cnt.get(id(sem), 0) + k
                self._cnt[id(sem)] = c
                self._tok = (sem, c)
                self._pend = None

        def _barrier(self):
            if self._pend is not None:
                self._auto_n += 1
                self._pend._instr.then_inc(self._auto, 1)
                self._tok = (self._auto, self._auto_n)
                self._pend = None
            if self._tok is not None:
                self._e.wait_ge(*self._tok)
                self._tok = None

        def wait_ge(self, sem, val):
            return self._e.wait_ge(sem, val)

        def __getattr__(self, name):
            real = getattr(self._e, name)

            def f(*a, **kw):
                self._barrier()
                instr = real(*a, **kw)
                p = _InstrProxy(self, instr)
                self._pend = p
                return p
            return f

    with nc.Block() as block:
        @block.tensor
        def _(t):
            for f in OPS["t"]:
                f(t)

        @block.vector
        def _(v):
            ae = AutoEdge(v, sVa.h)
            for f in OPS["v"]:
                f(ae)

        @block.scalar
        def _(s):
            ae = AutoEdge(s, sSa.h)
            for f in OPS["s"]:
                f(ae)

        @block.sync
        def _(q):
            for f in OPS["q"]:
                f(q)

        @block.gpsimd
        def _(g):
            for f in OPS["g"]:
                f(g)

    nc._keepalive = st
    return nc


_NC_CACHE = None


def _get_nc():
    global _NC_CACHE
    if _NC_CACHE is None:
        _NC_CACHE = build_program()
    return _NC_CACHE


def kernel(L0_attn_x, L0_attn_w_in, L0_attn_w_out, L0_attn_conv_w, L0_attn_conv_b,
           L0_attn_D, L0_ffn_router_weight, L0_ffn_router_bias, L0_ffn_gate_up_proj,
           L0_ffn_gate_up_proj_bias, L0_ffn_down_proj, L0_ffn_down_proj_bias):
    f32 = np.float32
    C = lambda a: np.ascontiguousarray(a, dtype=f32)

    x = np.asarray(L0_attn_x, dtype=f32).reshape(8192, D)
    w_in = np.asarray(L0_attn_w_in, dtype=f32)
    w_out = np.asarray(L0_attn_w_out, dtype=f32)
    conv_w = np.asarray(L0_attn_conv_w, dtype=f32)
    conv_b = np.asarray(L0_attn_conv_b, dtype=f32)
    Dvec = np.asarray(L0_attn_D, dtype=f32)
    rw = np.asarray(L0_ffn_router_weight, dtype=f32)
    rbias = np.asarray(L0_ffn_router_bias, dtype=f32)
    gup = np.asarray(L0_ffn_gate_up_proj, dtype=f32)
    gup_b = np.asarray(L0_ffn_gate_up_proj_bias, dtype=f32)
    down = np.asarray(L0_ffn_down_proj, dtype=f32)
    down_b = np.asarray(L0_ffn_down_proj_bias, dtype=f32)

    Wi = C(w_in.reshape(64, P, KC, P).transpose(0, 3, 2, 1))
    Wo = C(w_out.reshape(16, P, 2, 16, P).transpose(0, 2, 4, 3, 1))
    gate = gup[:, :, 0::2]
    up = gup[:, :, 1::2]
    G = C(gate.reshape(E, KC, P, 16, P).transpose(0, 3, 2, 1, 4))
    U = C(up.reshape(E, KC, P, 16, P).transpose(0, 3, 2, 1, 4))
    Dn = C(down.reshape(E, KC, P, 16, P).transpose(0, 3, 2, 1, 4))
    Rt = C(rw.reshape(E, KC, P).transpose(2, 1, 0))
    Rb = C(rbias.reshape(E, 1))
    Db = C(down_b)
    Bg = C(gup_b[:, 0::2].reshape(E, 16, P).transpose(2, 0, 1).reshape(P, 256))
    Bu = C(gup_b[:, 1::2].reshape(E, 16, P).transpose(2, 0, 1).reshape(P, 256))
    Cw = C(conv_w[:, 0, :].reshape(32, P, 4).transpose(1, 0, 2).reshape(P, P))
    Cb = C(conv_b.reshape(32, P).T)
    Dv = C(Dvec.reshape(32, P).T)

    xpad = np.concatenate([np.zeros((3, D), f32), x], axis=0)

    shared = dict(wi=Wi, wo=Wo, gw=G, uw=U, dw=Dn, rt=Rt, rb=Rb, db=Db,
                  bg=Bg, bu=Bu, cw=Cw, cb=Cb, dv=Dv)
    in_maps = []
    for c in range(8):
        m = dict(shared)
        m["xh"] = C(xpad[T * c:T * c + TH])
        in_maps.append(m)

    nc = _get_nc()
    res = run_bass_kernel_spmd(nc, in_maps, core_ids=list(range(8)))
    y = np.concatenate([res.results[c]["y"] for c in range(8)], axis=0)
    return y.reshape(1, 8192, D)
